# version 05
# baseline (speedup 1.0000x reference)
import contextlib
import math
import numpy as np
import ml_dtypes

import concourse.bass as bass
import concourse.mybir as mybir
from concourse.bass_utils import run_bass_kernel_spmd

F32 = mybir.dt.float32
BF16 = mybir.dt.bfloat16
AF = mybir.ActivationFunctionType
ALU = mybir.AluOpType
AX = mybir.AxisListType
BF = ml_dtypes.bfloat16

NT, D, NH, DH, NE, NS, HID = 4096, 2048, 16, 128, 16, 128, 512
NCH = 32
EPS = 1e-5
ENGS = ("sync", "gpsimd", "tensor", "vector", "scalar")
ISQ = float(1.0 / math.sqrt(DH))


def _rope_tables():
    inv_freq = (1.0 / np.power(np.float32(10000.0),
                               (np.arange(0, DH, 2, dtype=np.float32) / np.float32(DH)))).astype(np.float32)
    t = np.arange(NT, dtype=np.float32)
    freqs = t[:, None] * inv_freq[None, :]
    emb = np.concatenate([freqs, freqs], axis=1)
    cos_h = np.cos(emb)[:, ::2].astype(np.float32)
    sin_h = np.sin(emb)[:, ::2].astype(np.float32)
    idx = np.tile(np.arange(64), 2)
    return cos_h[:, idx].T.copy().astype(BF), sin_h[:, idx].T.copy().astype(BF)


def _mask_tiles():
    m = np.zeros((128, 4 * 512), dtype=np.float32)
    i = np.arange(128)[:, None]
    q = np.arange(512)[None, :]
    for rel in range(4):
        m[:, rel * 512:(rel + 1) * 512] = (rel * 128 + i <= q).astype(np.float32)
    return m.astype(BF)


class _Ctr:
    __slots__ = ("name", "v", "sem")

    def __init__(self, name):
        self.name, self.v, self.sem = name, 0, None


class _DummyInst:
    def then_inc(self, *a, **k):
        return self


_DINST = _DummyInst()


class _DummyEng:
    def __getattr__(self, name):
        def f(*a, **k):
            return _DINST
        return f


_DENG = _DummyEng()


class _VecHazardWrap:

    _OUTK = ("accum_out",)

    def __init__(self, eng, sem):
        self._e = eng
        self._sem = sem
        self._n = 0
        self._lr = []
        self._lw = []
        self._li = None
        self._li_sync = None

    @staticmethod
    def _iv(ap):
        try:
            name = ap.tensor.name
            lo = int(ap.offset)
            hi = lo
            for step, count in ap.ap:
                hi += int(step) * (int(count) - 1)
            return (name, min(lo, hi), max(lo, hi) + 1)
        except Exception:
            return None

    def __getattr__(self, fname):
        f = getattr(self._e, fname)
        if fname in ("wait_ge", "engine_nop", "nop", "drain") or not callable(f):
            return f
        wrap_self = self

        def call(*args, **kwargs):
            outs, ins = [], []
            for i, a in enumerate(args):
                if isinstance(a, bass.AP):
                    (outs if i == 0 else ins).append(a)
            for k, v in kwargs.items():
                if isinstance(v, bass.AP):
                    (outs if k in wrap_self._OUTK else ins).append(v)
            ivo = [x for x in map(wrap_self._iv, outs) if x]
            ivi = [x for x in map(wrap_self._iv, ins) if x]

            def clash(a, b):
                return any(x[0] == y[0] and x[1] < y[2] and y[1] < x[2]
                           for x in a for y in b)

            if (clash(wrap_self._lw, ivi) or clash(wrap_self._lw, ivo)
                    or clash(wrap_self._lr, ivo)):
                if wrap_self._li_sync is not None:
                    wrap_self._e.wait_ge(*wrap_self._li_sync)
                elif wrap_self._li is not None:
                    wrap_self._n += 1
                    wrap_self._li.then_inc(wrap_self._sem, 1)
                    wrap_self._e.wait_ge(wrap_self._sem, wrap_self._n)
                wrap_self._lr, wrap_self._lw = [], []
            inst = f(*args, **kwargs)
            wrap_self._lr, wrap_self._lw = ivi, ivo
            wrap_self._li = inst
            wrap_self._li_sync = None
            return inst

        return call


class Env:
    def __init__(self, real_engs):
        self.real = real_engs
        self.mode = "count"
        self.ev = {}
        self.e = {}
        self.ctrs = {}

    def ctr(self, name):
        c = _Ctr(name)
        self.ctrs[name] = c
        return c

    def begin(self, mode):
        self.mode = mode
        for c in self.ctrs.values():
            c.v = 0
        for n in ENGS:
            if mode == n:
                if n == "vector":
                    self.e[n] = _VecHazardWrap(self.real[n], self.vself_sem)
                else:
                    self.e[n] = self.real[n]
            else:
                self.e[n] = _DENG

    def inc(self, inst, ctr, k=1):
        ctr.v += k
        if inst is not _DINST:
            inst.then_inc(ctr.sem, k)
            w = self.e.get("vector")
            if isinstance(w, _VecHazardWrap) and w._li is inst:
                w._li_sync = (ctr.sem, ctr.v)
        return ctr.v

    def rec(self, key, ctr):
        if self.mode == "count":
            self.ev[key] = (ctr, ctr.v)
        else:
            c0, v0 = self.ev[key]
            assert c0 is ctr and v0 == ctr.v, (key, c0.name, v0, ctr.name, ctr.v)

    def wait(self, eng, _ctr_ignored, key=None):
        if key is None:
            key = _ctr_ignored
        if self.mode == "count":
            return
        ctr, v = self.ev[key]
        if v > 0 and self.e[eng] is not _DENG:
            self.e[eng].wait_ge(ctr.sem, v)


def build(dbg=False, upto=7):
    nc = bass.Bass("TRN2", target_bir_lowering=False, detect_race_conditions=False)
    skind = "ExternalOutput" if dbg else "Internal"

    x_d = nc.dram_tensor("x", [NT, D], F32, kind="ExternalInput").ap()
    wq_d = nc.dram_tensor("wq", [D, D], F32, kind="ExternalInput").ap()
    wk_d = nc.dram_tensor("wk", [D, D], F32, kind="ExternalInput").ap()
    wv_d = nc.dram_tensor("wv", [D, D], F32, kind="ExternalInput").ap()
    wo_d = nc.dram_tensor("wo", [D, D], F32, kind="ExternalInput").ap()
    gam_d = nc.dram_tensor("gam", [D], F32, kind="ExternalInput").ap()
    bet_d = nc.dram_tensor("bet", [D], F32, kind="ExternalInput").ap()
    mu_d = nc.dram_tensor("mu", [D, NE * NS], F32, kind="ExternalInput").ap()
    scl_d = nc.dram_tensor("scl", [1, 1], F32, kind="ExternalInput").ap()
    w1_d = nc.dram_tensor("w1", [NE, D, HID], F32, kind="ExternalInput").ap()
    b1_d = nc.dram_tensor("b1", [NE, HID], F32, kind="ExternalInput").ap()
    w2_d = nc.dram_tensor("w2", [NE, HID, D], F32, kind="ExternalInput").ap()
    b2_d = nc.dram_tensor("b2", [NE, D], F32, kind="ExternalInput").ap()
    ident_d = nc.dram_tensor("ident", [128, 128], BF16, kind="ExternalInput").ap()
    ropec_d = nc.dram_tensor("ropec", [128, NT], BF16, kind="ExternalInput").ap()
    ropes_d = nc.dram_tensor("ropes", [128, NT], BF16, kind="ExternalInput").ap()
    mask_d = nc.dram_tensor("mask", [128, 2048], BF16, kind="ExternalInput").ap()

    out_d = nc.dram_tensor("out", [NT, D], F32, kind="ExternalOutput").ap()

    wqp_d = nc.dram_tensor("wqp", [D, D], BF16, kind=skind).ap()
    wkp_d = nc.dram_tensor("wkp", [D, D], BF16, kind="Internal").ap()
    wvb_d = nc.dram_tensor("wvb", [D, D], BF16, kind="Internal").ap()
    wob_d = nc.dram_tensor("wob", [D, D], BF16, kind="Internal").ap()
    w1b_d = nc.dram_tensor("w1b", [NE, D, HID], BF16, kind="Internal").ap()
    w2b_d = nc.dram_tensor("w2b", [NE, HID, D], BF16, kind="Internal").ap()
    qt_d = nc.dram_tensor("qt", [NH, DH, NT], BF16, kind=skind).ap()
    kt_d = nc.dram_tensor("kt", [NH, DH, NT], BF16, kind=skind).ap()
    vb_d = nc.dram_tensor("vb", [NH, NT, DH], BF16, kind=skind).ap()
    atn_d = nc.dram_tensor("atn", [NCH, NH, DH, 128], BF16, kind=skind).ap()
    x1_d = nc.dram_tensor("x1h", [NT, D], F32, kind=skind).ap()
    xln_d = nc.dram_tensor("xlnh", [NT, D], BF16, kind=skind).ap()
    eh_d = nc.dram_tensor("eh", [NE, NT, NS], BF16, kind=skind).ap()
    soh_d = nc.dram_tensor("soh", [NE, NS, D], BF16, kind=skind).ap()
    rnh_d = nc.dram_tensor("rnh", [1, 2048], F32, kind="Internal").ap()

    with contextlib.ExitStack() as ctx:
        sb_ = lambda *a: ctx.enter_context(nc.sbuf_tensor(*a))
        ps_ = lambda *a: ctx.enter_context(nc.psum_tensor(*a))

        AR = sb_("AR", [128, 81920], BF16)
        ARF = sb_("ARF", [128, 10240], F32)
        ident = sb_("identt", [128, 128], BF16)
        stats = sb_("stats", [128, 32], F32)
        bn6 = sb_("bn6", [128, 24], F32)
        mv2 = sb_("mv2", [128, 4], F32)
        res_all = sb_("res_all", [128, 32], F32)
        rn_col = sb_("rn_col", [128, 16], F32)
        sn_acc = sb_("sn_acc", [128, 16], F32)
        rl_sb = sb_("rl_sb", [128, 4], F32)
        ones1_bf = sb_("ones1_bf", [1, 128], BF16)
        onesc_bf = sb_("onesc_bf", [128, 1], BF16)
        ones1_f = sb_("ones1_f", [1, 128], F32)
        onesc_f = sb_("onesc_f", [128, 1], F32)
        sc_scl = sb_("sc_scl", [128, 1], F32)
        sc_nab = sb_("sc_nab", [128, 1], F32)
        s_sc = sb_("s_sc", [1, 2], F32)
        eps128 = sb_("eps128", [128, 1], F32)
        eps2_128 = sb_("eps2_128", [128, 1], F32)
        hpre = None

        pmA = [ps_(f"pmA{i}", [128, 512], F32) for i in range(4)]
        pmB = [ps_(f"pmB{i}", [128, 512], F32) for i in range(2)]
        pmT = [ps_(f"pmT{i}", [128, 512], F32) for i in range(2)]
        pmT_bf = [p[:].bitcast(BF16) for p in pmT]

        SEM_NAMES = ("PE", "SC", "VE",
                     "CON", "WTL", "X0", "X1", "HB0", "HB1", "AB0", "AB1", "EBL",
                     "WQL0", "WQL1", "EBT0", "EBT1", "BLK",
                     "PQ", "PK", "PV", "PO", "PW1", "PW2", "MUB", "BLD",
                     "ROT0", "ROT1", "LNB0", "LNB1", "ATS0", "ATS1",
                     "X1S0", "X1S1", "XLS", "EHS", "SOS", "OUT0", "OUT1", "VSELF")
        sems = {n: ctx.enter_context(nc.semaphore(n)) for n in SEM_NAMES}
        real = {"sync": nc.sync, "gpsimd": nc.gpsimd, "tensor": nc.tensor,
                "vector": nc.vector, "scalar": nc.scalar}
        P = Env(real)
        C = {}
        for n in SEM_NAMES:
            C[n] = P.ctr(n)
            C[n].sem = sems[n]
        cPE, cSC, cVE = C["PE"], C["SC"], C["VE"]
        P.vself_sem = sems["VSELF"]
        cLD = cGP = None
        cX = [C["X0"], C["X1"]]
        cHB = [C["HB0"], C["HB1"]]
        cAB = [C["AB0"], C["AB1"]]
        cWQL = [C["WQL0"], C["WQL1"]]
        cEBT = [C["EBT0"], C["EBT1"]]
        cROT = [C["ROT0"], C["ROT1"]]
        cLNB = [C["LNB0"], C["LNB1"]]
        cATS = [C["ATS0"], C["ATS1"]]
        cX1S = [C["X1S0"], C["X1S1"]]
        cOUT = [C["OUT0"], C["OUT1"]]

        def arr(lo, n):
            return AR[:, lo:lo + n]

        def arf(lo, n):
            return ARF[:, lo:lo + n]

        lnT = [arr(dc * 2048, 2048) for dc in range(16)]
        WT = [arr(32768 + dc * 2048, 2048) for dc in range(16)]
        lnbf = [arr(65536 + i * 2048, 2048) for i in range(2)]
        qeqo = [arr(69632 + i * 1024, 1024) for i in range(2)]
        rot = [arr(71680 + i * 1024, 1024) for i in range(2)]
        ropeC = arr(73728, 4096)
        ropeS = arr(77824, 4096)
        masks = arr(0, 2048)
        Qb = [arr(2048 + i * 4096, 4096) for i in range(2)]
        Kb = [arr(10240 + i * 4096, 4096) for i in range(2)]
        Vaug = [arr(18432 + i * 4128, 4128) for i in range(2)]
        Pt = [arr(26688 + i * 512, 512) for i in range(2)]
        accbf = [arr(27712 + i * 128, 128) for i in range(4)]
        atst = [arr(28224 + i * 128, 128) for i in range(8)]
        woT = WT
        ab2 = [arr(i * 2048, 2048) for i in range(2)]
        xlnres = [arr(c * 2048, 2048) for c in range(32)]
        must = [arr(40960 + i * 2048, 2048) for i in range(2)]
        esb = arr(40960, 2048)
        xnT = [arr(43008 + dc * 128, 128) for dc in range(16)]
        xlnst = arr(45056, 2048)
        xnbf = arr(47104, 2048)
        mun = [arr(49152 + dc * 2048, 2048) for dc in range(16)]
        eb3 = arr(65536, 4096)
        w1q = [arr(69632 + i * 2048, 2048) for i in range(2)]
        w2q = [arr(73728 + i * 2048, 2048) for i in range(2)]
        slin = arr(77824, 2048)
        hbf = arr(77824, 512)
        sobf = arr(77824, 2048)
        siT = [arr(79872 + dc * 128, 128) for dc in range(16)]
        hT = [arr(79872 + i * 128, 128) for i in range(4)]
        sores = [arr(ec * 2048, 2048) for ec in range(16)]
        ebT = [[arr(32768 + b * 16384 + ec * 1024, 1024) for ec in range(16)] for b in range(2)]
        xbf = [arf(i * 2048, 2048) for i in range(2)]
        sqscr = arf(4096, 2048)
        Greg = arf(4096, 2048)
        lnscr = arf(6144, 2048)
        Breg = arf(8192, 2048)
        rnb = arf(0, 2048)
        rnrow = arf(6144, 2048)
        osb = [arf(4096 + i * 2048, 2048) for i in range(2)]
        hpre = arf(0, 512)
        B1f = arf(512, 512)
        B2f = arf(1024, 2048)

        def SY():
            return P.e["sync"]

        def GP():
            return P.e["gpsimd"]

        def TE():
            return P.e["tensor"]

        def VE():
            return P.e["vector"]

        def SC():
            return P.e["scalar"]

        def dma(engf, ctr, out, in_, **kw):
            return P.inc(engf().dma_start(out, in_, **kw), ctr, 16)

        def vfence(src):
            return VE().tensor_copy(stats[:, 31:32], src)

        def sfence(src):
            return SC().copy(stats[:, 30:31], src)

        def vmargin():
            for _ in range(4):
                VE().engine_nop()

        def p0_prep():
            wq5 = wq_d.rearrange("(p hh j t) d -> p t hh j d", p=8, hh=2, j=64, t=2)
            wk5 = wk_d.rearrange("(p hh j t) d -> p t hh j d", p=8, hh=2, j=64, t=2)
            wqp5 = wqp_d.rearrange("(p t hh j) d -> p t hh j d", p=8, t=2, hh=2, j=64)
            wkp5 = wkp_d.rearrange("(p t hh j) d -> p t hh j d", p=8, t=2, hh=2, j=64)
            for i in range(8):
                dma(GP, C["PQ"], wqp5[i], wq5[i])
            P.rec("prep.wq", C["PQ"])
            for i in range(8):
                dma(GP, C["PK"], wkp5[i], wk5[i])
            P.rec("prep.wk", C["PK"])
            for i in range(4):
                dma(GP, C["PV"], wvb_d[i * 512:(i + 1) * 512, :], wv_d[i * 512:(i + 1) * 512, :])
            P.rec("prep.wv", C["PV"])
            for i in range(4):
                dma(GP, C["PO"], wob_d[i * 512:(i + 1) * 512, :], wo_d[i * 512:(i + 1) * 512, :])
            P.rec("prep.wo", C["PO"])
            for e in range(NE):
                dma(GP, C["PW1"], w1b_d[e], w1_d[e])
            P.rec("prep.w1", C["PW1"])
            for e in range(NE):
                dma(GP, C["PW2"], w2b_d[e], w2_d[e])
            P.rec("prep.w2", C["PW2"])

            dma(SY, C["CON"], ident[:], ident_d)
            dma(SY, C["CON"], ropeC, ropec_d)
            dma(SY, C["CON"], ropeS, ropes_d)
            dma(SY, C["CON"], s_sc[:, 0:1], scl_d)
            P.rec("p0.consts", C["CON"])

            VE().memset(ones1_bf[:], 1.0)
            VE().memset(onesc_bf[:], 1.0)
            VE().memset(ones1_f[:], 1.0)
            VE().memset(eps128[:], float(EPS))
            VE().memset(eps2_128[:], float(EPS * EPS))
            P.inc(VE().memset(onesc_f[:], 1.0), cVE, 1)
            P.rec("p0.ones", cVE)

            P.wait("scalar", cLD, "p0.consts")
            P.inc(SC().activation(s_sc[:, 1:2], s_sc[:, 0:1], AF.Abs), cSC, 1)
            P.rec("p0.abs", cSC)
            P.wait("tensor", cSC, "p0.abs")
            P.wait("tensor", cVE, "p0.ones")
            TE().matmul(pmA[0][:, 0:1], ones1_f[:], s_sc[:, 1:2], start=True, stop=True)
            P.inc(TE().matmul(pmA[0][:, 1:2], ones1_f[:], s_sc[:, 0:1], start=True, stop=True), cPE, 1)
            P.rec("p0.sclmm", cPE)
            P.wait("vector", cPE, "p0.sclmm")
            VE().tensor_scalar(sc_nab[:], pmA[0][:, 0:1], -1.0, None, ALU.mult)
            VE().tensor_copy(sc_scl[:], pmA[0][:, 1:2])
            P.inc(vfence(sc_scl[:]), cVE, 1)
            P.rec("p0.scl", cVE)

        def p1():
            for half in range(2):
                tok0 = half * 2048
                for c in range(16):
                    gc = half * 16 + c
                    if gc >= 2:
                        P.wait("sync", cVE, f"p1.xcons.{gc - 2}")
                    dma(SY, cX[gc % 2], xbf[c % 2], x_d[gc * 128:(gc + 1) * 128, :])
                    P.rec(f"p1.xld.{gc}", cX[gc % 2])

                    P.wait("vector", cLD, f"p1.xld.{gc}")
                    if gc == 0:
                        P.wait("vector", None, "p1.xld.1")
                    if gc >= 2:
                        P.wait("vector", cSC, f"p1.rs.{gc - 2}")
                    if gc == 16:
                        P.wait("vector", None, "p1.vst.0.14")
                    if gc == 17:
                        P.wait("vector", None, "p1.vst.0.15")
                    vmargin()
                    VE().reduce_sum(mv2[:, 0:1], xbf[c % 2], AX.X)
                    VE().tensor_tensor(lnscr, xbf[c % 2], xbf[c % 2], ALU.mult)
                    VE().reduce_sum(mv2[:, 1:2], lnscr, AX.X)
                    VE().tensor_scalar(mv2[:, 0:1], mv2[:, 0:1], float(1.0 / D), None, ALU.mult)
                    VE().tensor_scalar(mv2[:, 1:2], mv2[:, 1:2], float(1.0 / D), None, ALU.mult)
                    VE().tensor_tensor(stats[:, 7:8], mv2[:, 0:1], mv2[:, 0:1], ALU.mult)
                    VE().tensor_tensor(mv2[:, 1:2], mv2[:, 1:2], stats[:, 7:8], ALU.subtract)
                    P.inc(vfence(mv2[:, 1:2]), cVE, 1)
                    P.rec(f"p1.mv.{gc}", cVE)

                    P.wait("scalar", cVE, f"p1.mv.{gc}")
                    if gc == 16:
                        P.wait("scalar", cPE, "p1.passend.0.v")
                    SC().activation(stats[:, 0:1], mv2[:, 1:2], AF.Ln, bias=eps128[:])
                    SC().activation(stats[:, 1:2], stats[:, 0:1], AF.Exp, scale=-0.5)
                    P.inc(sfence(stats[:, 1:2]), cSC, 1)
                    P.rec(f"p1.rs.{gc}", cSC)

                    P.wait("vector", cSC, f"p1.rs.{gc}")
                    if gc >= 2:
                        P.wait("vector", cPE, f"p1.tr16.{gc - 2}")
                    VE().tensor_scalar(lnbf[c % 2], xbf[c % 2], mv2[:, 0:1],
                                       stats[:, 1:2], ALU.subtract, ALU.mult)
                    P.inc(vfence(lnbf[c % 2][:, 2047:2048]), cVE, 1)
                    P.rec(f"p1.xcons.{gc}", cVE)

                    P.wait("tensor", cVE, f"p1.xcons.{gc}")
                    if gc == 0:
                        P.wait("tensor", cLD, "p0.consts")
                    for dc in range(16):
                        k = gc * 16 + dc
                        if k >= 2:
                            P.wait("tensor", cSC, f"p1.lnev.{k - 2}")
                        P.inc(TE().transpose(pmT_bf[k % 2][:, 0:128],
                                             lnbf[c % 2][:, dc * 128:(dc + 1) * 128], ident[:]), cPE, 1)
                        P.rec(f"p1.tr.{k}", cPE)
                        P.wait("scalar", cPE, f"p1.tr.{k}")
                        P.inc(SC().copy(lnT[dc][:, c * 128:(c + 1) * 128], pmT_bf[k % 2][:, 0:128]), cSC, 1)
                        P.rec(f"p1.lnev.{k}", cSC)
                    P.rec(f"p1.tr16.{gc}", cPE)
                P.rec(f"p1.lnall.{half}", cSC)

                for wname, wsrc, dst in (("q", wqp_d, qt_d), ("k", wkp_d, kt_d)):
                    P.wait("sync", cGP, f"prep.w{wname}")
                    prev = {("q", 0): None, ("k", 0): "p1.passend.0.q",
                            ("q", 1): "p1.passend.0.v", ("k", 1): "p1.passend.1.q"}[(wname, half)]
                    if prev:
                        P.wait("sync", cPE, prev)
                    for dc in range(16):
                        P.inc(SY().dma_start_transpose(WT[dc], wsrc[:, dc * 128:(dc + 1) * 128]), C["WTL"], 16)
                    P.rec(f"p1.wt.{half}.{wname}", C["WTL"])
                    for g in range(4):
                        for a in range(8):
                            u = g * 8 + a
                            pq = (pmB[0], pmB[1]) if a % 2 == 0 else (pmA[0], pmA[1])
                            if u == 0:
                                P.wait("tensor", cLD, f"p1.wt.{half}.{wname}")
                                P.wait("tensor", cSC, f"p1.lnall.{half}")
                                if half == 0 and wname == "q":
                                    P.wait("tensor", cVE, "p0.scl")
                            if u >= 2:
                                P.wait("tensor", cSC, f"p1.ev.{wname}.{half}.{u - 2}")
                            for dc in range(16):
                                inst = TE().matmul(pq[0][:], WT[dc][:, (2 * a) * 128:(2 * a + 1) * 128],
                                                   lnT[dc][:, g * 512:(g + 1) * 512],
                                                   start=(dc == 0), stop=(dc == 15))
                                if dc == 15:
                                    P.inc(inst, cPE, 1)
                            P.rec(f"p1.mm0.{wname}.{half}.{u}", cPE)
                            for dc in range(16):
                                inst = TE().matmul(pq[1][:], WT[dc][:, (2 * a + 1) * 128:(2 * a + 2) * 128],
                                                   lnT[dc][:, g * 512:(g + 1) * 512],
                                                   start=(dc == 0), stop=(dc == 15))
                                if dc == 15:
                                    P.inc(inst, cPE, 1)
                            P.rec(f"p1.mm1.{wname}.{half}.{u}", cPE)

                            P.wait("scalar", cPE, f"p1.mm1.{wname}.{half}.{u}")
                            if u >= 2:
                                P.wait("scalar", cVE, f"p1.rope.{wname}.{half}.{u - 2}")
                            SC().copy(qeqo[a % 2][:, 0:512], pq[0][:])
                            P.inc(SC().copy(qeqo[a % 2][:, 512:1024], pq[1][:]), cSC, 1)
                            P.rec(f"p1.ev.{wname}.{half}.{u}", cSC)

                            P.wait("vector", cSC, f"p1.ev.{wname}.{half}.{u}")
                            if u >= 2:
                                P.wait("vector", cGP, f"p1.st.{wname}.{half}.{u - 2}")
                            if half == 0 and wname == "q" and u == 0:
                                P.wait("vector", cLD, "p0.consts")
                            ts = slice(tok0 + g * 512, tok0 + (g + 1) * 512)
                            qe = qeqo[a % 2][:, 0:512]
                            qo = qeqo[a % 2][:, 512:1024]
                            t1 = lnbf[0][:, 0:512]
                            t2 = lnbf[0][:, 512:1024]
                            VE().tensor_tensor(t1, qe, ropeC[:, ts], ALU.mult)
                            VE().tensor_tensor(t2, qo, ropeS[:, ts], ALU.mult)
                            VE().tensor_tensor(rot[a % 2][:, 0:512], t1, t2, ALU.subtract)
                            VE().tensor_tensor(t1, qo, ropeC[:, ts], ALU.mult)
                            VE().tensor_tensor(t2, qe, ropeS[:, ts], ALU.mult)
                            P.inc(VE().tensor_tensor(rot[a % 2][:, 512:1024], t1, t2, ALU.add), cVE, 1)
                            P.rec(f"p1.rope.{wname}.{half}.{u}", cVE)

                            P.wait("gpsimd", cVE, f"p1.rope.{wname}.{half}.{u}")
                            dma(GP, cROT[u % 2], dst[2 * a, 0:64, ts], rot[a % 2][0:64, 0:512])
                            dma(GP, cROT[u % 2], dst[2 * a + 1, 0:64, ts], rot[a % 2][64:128, 0:512])
                            dma(GP, cROT[u % 2], dst[2 * a, 64:128, ts], rot[a % 2][0:64, 512:1024])
                            dma(GP, cROT[u % 2], dst[2 * a + 1, 64:128, ts], rot[a % 2][64:128, 512:1024])
                            P.rec(f"p1.st.{wname}.{half}.{u}", cROT[u % 2])
                    P.rec(f"p1.passend.{half}.{wname}", cPE)

                P.wait("sync", cGP, "prep.wv")
                P.wait("sync", cPE, f"p1.passend.{half}.k")
                for dc in range(16):
                    P.inc(SY().dma_start_transpose(WT[dc], wvb_d[:, dc * 128:(dc + 1) * 128]), C["WTL"], 16)
                P.rec(f"p1.wt.{half}.v", C["WTL"])
                for cc in range(16):
                    gc = half * 16 + cc
                    for s4 in range(4):
                        k = cc * 4 + s4
                        if k == 0:
                            P.wait("tensor", cLD, f"p1.wt.{half}.v")
                        if k >= 2:
                            P.wait("tensor", cSC, f"p1.vev.{half}.{k - 2}")
                        for dc in range(16):
                            inst = TE().matmul(pmB[k % 2][:], lnT[dc][:, cc * 128:(cc + 1) * 128],
                                               WT[dc][:, s4 * 512:(s4 + 1) * 512],
                                               start=(dc == 0), stop=(dc == 15))
                            if dc == 15:
                                P.inc(inst, cPE, 1)
                        P.rec(f"p1.vmm.{half}.{k}", cPE)
                        P.wait("scalar", cPE, f"p1.vmm.{half}.{k}")
                        if s4 == 0 and cc == 0:
                            P.wait("scalar", cVE, f"p1.rope.k.{half}.31")
                        if s4 == 0 and cc >= 2:
                            P.wait("scalar", None, f"p1.vst.{half}.{cc - 2}")
                        P.inc(SC().copy(lnbf[cc % 2][:, s4 * 512:(s4 + 1) * 512], pmB[k % 2][:]), cSC, 1)
                        P.rec(f"p1.vev.{half}.{k}", cSC)
                    P.wait("gpsimd", cSC, f"p1.vev.{half}.{cc * 4 + 3}")
                    for h in range(NH):
                        dma(GP, cLNB[cc % 2], vb_d[h, gc * 128:(gc + 1) * 128, :],
                            lnbf[cc % 2][:, h * 128:(h + 1) * 128])
                    P.rec(f"p1.vst.{half}.{cc}", cLNB[cc % 2])
                P.rec(f"p1.passend.{half}.v", cPE)
            P.rec("p1.done.pe", cPE)
            P.rec("p1.done.gp", cROT[1])

        def p2():
            P.wait("sync", cPE, "p1.done.pe")
            dma(SY, C["CON"], masks, mask_d)
            P.rec("p2.mask", C["CON"])
            P.wait("vector", cPE, "p1.done.pe")
            for hb in range(2):
                va = Vaug[hb].rearrange("p (c w) -> p c w", w=129)
                P.inc(VE().memset(va[:, :, 128:129], 1.0), cVE, 1)
            P.rec("p2.vones", cVE)

            it = 0
            pending_endgame = [None]
            for h in range(NH):
                hb = h % 2
                if h >= 2:
                    P.wait("sync", cPE, f"p2.hdone.{h - 2}")
                else:
                    P.wait("sync", None, "p1.st.k.1.31")
                    P.wait("sync", None, "p1.st.k.1.30")
                    P.wait("sync", None, "p1.vst.1.15")
                    P.wait("sync", None, "p1.vst.1.14")
                    P.wait("sync", cVE, "p2.vones")
                dma(SY, cHB[hb], Qb[hb], qt_d[h])
                dma(SY, cHB[hb], Kb[hb], kt_d[h])
                va = Vaug[hb].rearrange("p (c w) -> p c w", w=129)
                vsrc = vb_d[h].rearrange("(c p) e -> p c e", p=128)
                for i in range(8):
                    dma(SY, cHB[hb], va[:, i * 4:(i + 1) * 4, 0:128], vsrc[:, i * 4:(i + 1) * 4, :])
                P.rec(f"p2.hld.{h}", cHB[hb])

                for qb in range(8):
                    kmax = 4 * qb + 4
                    gqb = h * 8 + qb
                    it0 = it

                    def emit_S(itx, kkx):
                        six = itx % 2
                        if kkx == 0:
                            P.wait("tensor", cLD, f"p2.hld.{h}")
                            if itx == 0:
                                P.wait("tensor", cSC, "p1.vev.1.63")
                                P.wait("tensor", cSC, "p1.ev.k.1.31")
                        if itx >= 2:
                            P.wait("tensor", cSC, f"p2.exp.{itx - 2}")
                        P.inc(TE().matmul(pmB[six][:], Kb[hb][:, kkx * 128:(kkx + 1) * 128],
                                          Qb[hb][:, qb * 512:(qb + 1) * 512],
                                          start=True, stop=True), cPE, 1)
                        P.rec(f"p2.smm.{itx}", cPE)

                    def emit_exp(itx, kkx):
                        six = itx % 2
                        P.wait("scalar", cPE, f"p2.smm.{itx}")
                        if itx >= 2:
                            P.wait("scalar", cPE, f"p2.pv.{itx - 2}")
                        P.inc(SC().activation(Pt[six], pmB[six][:], AF.Exp, scale=ISQ), cSC, 1)
                        P.rec(f"p2.exp.{itx}", cSC)
                        if kkx >= 4 * qb:
                            rel = kkx - 4 * qb
                            if itx == 0:
                                P.wait("vector", None, "p2.mask")
                            P.wait("vector", cSC, f"p2.exp.{itx}")
                            P.inc(VE().tensor_tensor(Pt[six], Pt[six],
                                                     masks[:, rel * 512:(rel + 1) * 512], ALU.mult), cVE, 1)
                            P.rec(f"p2.msk.{itx}", cVE)

                    for j in range(min(2, kmax)):
                        emit_S(it0 + j, j)
                    if pending_endgame[0] is not None:
                        pending_endgame[0]()
                        pending_endgame[0] = None
                    for kk in range(kmax):
                        itx = it0 + kk
                        si = it % 2
                        masked = kk >= 4 * qb
                        emit_exp(itx, kk)
                        if kk == 0 and gqb >= 1:
                            P.wait("tensor", cVE, f"p2.acc.{gqb - 1}")
                        if masked:
                            P.wait("tensor", cVE, f"p2.msk.{itx}")
                        else:
                            P.wait("tensor", cSC, f"p2.exp.{itx}")
                        for qs in range(4):
                            inst = TE().matmul(pmA[qs][:, 0:129], Pt[si][:, qs * 128:(qs + 1) * 128],
                                               Vaug[hb][:, kk * 129:(kk + 1) * 129],
                                               start=(kk == 0), stop=(kk == kmax - 1))
                            if qs == 3:
                                P.inc(inst, cPE, 1)
                        P.rec(f"p2.pv.{itx}", cPE)
                        if kk + 2 < kmax:
                            emit_S(it0 + kk + 2, kk + 2)
                        it += 1
                    def mk_endgame(gqb=gqb, qb=qb, h=h, last_it=it - 1):
                        def endgame():
                            P.wait("vector", cPE, f"p2.pv.{last_it}")
                            if gqb >= 1:
                                P.wait("vector", cPE, f"p2.tr.{(gqb - 1) * 4 + 3}")
                            for qs in range(4):
                                VE().reciprocal(rl_sb[:, qs:qs + 1], pmA[qs][:, 128:129])
                                VE().tensor_scalar(accbf[qs], pmA[qs][:, 0:128],
                                                   rl_sb[:, qs:qs + 1], None, ALU.mult)
                            P.inc(vfence(accbf[3][:, 127:128]), cVE, 1)
                            P.rec(f"p2.acc.{gqb}", cVE)
                            for qs in range(4):
                                kq = gqb * 4 + qs
                                tb = qs % 2
                                if qs == 0:
                                    P.wait("tensor", cVE, f"p2.acc.{gqb}")
                                    if gqb == 0:
                                        P.wait("tensor", cSC, "p1.lnev.511")
                                if kq >= 2:
                                    P.wait("tensor", cVE, f"p2.trev.{kq - 2}")
                                P.inc(TE().transpose(pmT_bf[tb][:, 0:128], accbf[qs], ident[:]), cPE, 1)
                                P.rec(f"p2.tr.{kq}", cPE)
                                P.wait("vector", cPE, f"p2.tr.{kq}")
                                if qs == 0 and gqb >= 2:
                                    P.wait("vector", None, f"p2.ast.{gqb - 2}")
                                P.inc(VE().tensor_copy(atst[(qb % 2) * 4 + qs],
                                                       pmT_bf[tb][:, 0:128]), cVE, 1)
                                P.rec(f"p2.trev.{kq}", cVE)
                            P.wait("gpsimd", cVE, f"p2.trev.{gqb * 4 + 3}")
                            for qs in range(4):
                                dma(GP, cATS[qb % 2], atn_d[qb * 4 + qs, h], atst[(qb % 2) * 4 + qs])
                            P.rec(f"p2.ast.{gqb}", cATS[qb % 2])
                        return endgame
                    pending_endgame[0] = mk_endgame()
                P.rec(f"p2.hdone.{h}", cPE)
            if pending_endgame[0] is not None:
                pending_endgame[0]()
                pending_endgame[0] = None

        def p2b():
            P.wait("sync", cGP, "prep.wo")
            P.wait("sync", cPE, "p1.done.pe")
            for dc in range(16):
                P.inc(SY().dma_start_transpose(woT[dc], wob_d[:, dc * 128:(dc + 1) * 128]), C["WTL"], 16)
            P.rec("p2b.wt", C["WTL"])
            for c in range(NCH):
                P.wait("sync", cGP, f"p2.ast.{15 * 8 + c // 4}")
                if c < 2:
                    P.wait("sync", cPE, "p2.hdone.15")
                    P.wait("sync", cVE, "p2.msk.2303")
                    P.wait("sync", None, f"p1.xcons.{30 + c}")
                else:
                    P.wait("sync", cPE, f"p2b.mm.{(c - 2) * 4 + 3}")
                    P.wait("sync", None, f"p2b.st.{c - 2}")
                src = atn_d[c].rearrange("h p q -> p h q")
                dst = ab2[c % 2].rearrange("p (h q) -> p h q", h=16)
                for i in range(4):
                    dma(SY, cAB[c % 2], dst[:, i * 4:(i + 1) * 4, :], src[:, i * 4:(i + 1) * 4, :])
                P.rec(f"p2b.abld.{c}", cAB[c % 2])
                dma(SY, cX[c % 2], xbf[c % 2], x_d[c * 128:(c + 1) * 128, :])
                P.rec(f"p2b.xld.{c}", cX[c % 2])

                for hc in range(16):
                    if hc == 0:
                        P.wait("tensor", None, f"p2b.abld.{c}")
                        P.wait("tensor", None, f"p2b.xld.{c}")
                        if c == 0:
                            P.wait("tensor", None, "p2b.wt")
                            P.wait("tensor", cVE, "p2.acc.127")
                            P.wait("tensor", None, "p2.ast.127")
                        else:
                            P.wait("tensor", cVE, f"p2b.ev.{(c - 1) * 4 + 3}")
                    for s4 in range(4):
                        inst = TE().matmul(pmA[s4][:], ab2[c % 2][:, hc * 128:(hc + 1) * 128],
                                           woT[hc][:, s4 * 512:(s4 + 1) * 512],
                                           start=(hc == 0), stop=(hc == 15))
                        if s4 == 3 and hc == 15:
                            P.inc(inst, cPE, 1)
                P.rec(f"p2b.mm.{c * 4 + 3}", cPE)
                for s4 in range(4):
                    k = c * 4 + s4
                    if s4 == 0:
                        P.wait("vector", cPE, f"p2b.mm.{c * 4 + 3}")
                    P.inc(VE().tensor_tensor(xbf[c % 2][:, s4 * 512:(s4 + 1) * 512], pmA[s4][:],
                                             xbf[c % 2][:, s4 * 512:(s4 + 1) * 512], ALU.add), cVE, 1)
                    P.rec(f"p2b.ev.{k}", cVE)
                P.wait("gpsimd", cVE, f"p2b.ev.{c * 4 + 3}")
                dma(GP, cX1S[c % 2], x1_d[c * 128:(c + 1) * 128, :], xbf[c % 2])
                P.rec(f"p2b.st.{c}", cX1S[c % 2])
            P.rec("p2b.done.pe", cPE)

        def p3a():
            P.wait("sync", cPE, "p2b.done.pe")
            P.wait("sync", None, "p2b.st.30")
            P.wait("sync", None, "p2b.st.31")
            P.wait("gpsimd", cPE, "p2b.done.pe")
            for dc in range(16):
                if dc >= 2:
                    P.wait("sync", cVE, f"p3a.musqv.{dc - 2}")
                dma(SY, cX[dc % 2], xbf[dc % 2], mu_d[dc * 128:(dc + 1) * 128, :])
                P.rec(f"p3a.muf.{dc}", cX[dc % 2])
                dma(GP, C["MUB"], mun[dc], mu_d[dc * 128:(dc + 1) * 128, :])
                P.rec(f"p3a.mub.{dc}", C["MUB"])
                P.wait("vector", cLD, f"p3a.muf.{dc}")
                if dc >= 1:
                    P.wait("vector", cPE, f"p3a.musq.{dc - 1}")
                if dc == 0:
                    P.wait("vector", cPE, "p2b.done.pe")
                P.inc(VE().tensor_tensor(sqscr, xbf[dc % 2], xbf[dc % 2], ALU.mult), cVE, 1)
                P.rec(f"p3a.musqv.{dc}", cVE)
                P.wait("tensor", cVE, f"p3a.musqv.{dc}")
                if dc == 0:
                    P.wait("tensor", cVE, "p2.acc.127")
                    P.wait("tensor", cVE, "p2b.ev.127")
                    P.wait("tensor", cSC, "p2.exp.2303")
                for s4 in range(4):
                    tgt = (pmA[1], pmA[2], pmA[3], pmB[0])[s4]
                    inst = TE().matmul(tgt[0:1, :], onesc_f[:], sqscr[:, s4 * 512:(s4 + 1) * 512],
                                       start=(dc == 0), stop=(dc == 15))
                    if s4 == 3:
                        P.inc(inst, cPE, 1)
                P.rec(f"p3a.musq.{dc}", cPE)
            P.wait("scalar", cPE, "p3a.musq.15")
            for s4 in range(4):
                src = (pmA[1], pmA[2], pmA[3], pmB[0])[s4]
                SC().activation(rnrow[0:1, s4 * 512:(s4 + 1) * 512], src[0:1, :], AF.Ln)
                P.inc(SC().activation(rnrow[0:1, s4 * 512:(s4 + 1) * 512],
                                      rnrow[0:1, s4 * 512:(s4 + 1) * 512], AF.Exp, scale=0.5), cSC, 1)
            P.rec("p3a.nrm", cSC)
            P.wait("vector", cSC, "p3a.nrm")
            VE().tensor_scalar(rnrow[0:1, :], rnrow[0:1, :], 1e-12, None, ALU.max)
            VE().reciprocal(rnrow[0:1, :], rnrow[0:1, :])
            P.inc(VE().tensor_copy(stats[0:1, 31:32], rnrow[0:1, 2047:2048]), cVE, 1)
            P.rec("p3a.rn", cVE)
            P.wait("gpsimd", cVE, "p3a.rn")
            dma(GP, C["MUB"], rnh_d, rnrow[0:1, :])
            P.rec("p3a.rnst", C["MUB"])
            P.wait("sync", None, "p3a.rnst")
            dma(SY, C["BLK"], rnb, rnh_d.broadcast_to((128, 2048)))
            P.rec("p3a.rnld", C["BLK"])
            P.wait("vector", None, "p3a.rnld")
            P.wait("vector", None, "p3a.mub.15")
            for dc in range(16):
                P.inc(VE().tensor_tensor(mun[dc], mun[dc], rnb, ALU.mult), cVE, 1)
            P.rec("p3a.mun", cVE)
            P.wait("sync", cPE, "p3a.musq.15")
            dma(SY, C["CON"], Greg, gam_d.unsqueeze(0).broadcast_to((128, D)))
            dma(SY, C["CON"], Breg, bet_d.unsqueeze(0).broadcast_to((128, D)))
            P.rec("p3a.gb", C["CON"])

            for c in range(NCH):
                if c >= 2:
                    P.wait("sync", cVE, f"p3a.xcons.{c - 2}")
                else:
                    P.wait("sync", cVE, "p3a.mun")
                dma(SY, cX[c % 2], xbf[c % 2], x1_d[c * 128:(c + 1) * 128, :])
                P.rec(f"p3a.xld.{c}", cX[c % 2])

                P.wait("scalar", cLD, f"p3a.xld.{c}")
                if c >= 1:
                    P.wait("scalar", cGP, f"p3a.est.{c - 1}")
                    P.wait("scalar", cVE, f"p3a.res.{c - 1}")
                P.inc(SC().activation(esb, xbf[c % 2], AF.Copy,
                                      accum_out=mv2[:, 0:1]), cSC, 1)
                P.rec(f"p3a.sx.{c}", cSC)

                P.wait("vector", cLD, f"p3a.xld.{c}")
                if c == 0:
                    P.wait("vector", cLD, "p3a.gb")
                if c >= 2:
                    P.wait("vector", cSC, f"p3a.rs.{c - 2}")
                vmargin()
                VE().tensor_tensor(lnscr, xbf[c % 2], xbf[c % 2], ALU.mult)
                VE().reduce_sum(mv2[:, 1:2], lnscr, AX.X)
                P.wait("vector", cSC, f"p3a.sx.{c}")
                VE().tensor_scalar(mv2[:, 0:1], mv2[:, 0:1], float(1.0 / D), None, ALU.mult)
                VE().tensor_scalar(mv2[:, 1:2], mv2[:, 1:2], float(1.0 / D), None, ALU.mult)
                VE().tensor_tensor(stats[:, 7:8], mv2[:, 0:1], mv2[:, 0:1], ALU.mult)
                VE().tensor_tensor(mv2[:, 1:2], mv2[:, 1:2], stats[:, 7:8], ALU.subtract)
                P.inc(vfence(mv2[:, 1:2]), cVE, 1)
                P.rec(f"p3a.mv.{c}", cVE)

                P.wait("scalar", cVE, f"p3a.mv.{c}")
                SC().activation(stats[:, 0:1], mv2[:, 1:2], AF.Ln, scale=float(1.0 + EPS),
                                bias=eps2_128[:])
                SC().activation(stats[:, 6:7], stats[:, 0:1], AF.Exp, scale=-0.5)
                P.inc(sfence(stats[:, 6:7]), cSC, 1)
                P.rec(f"p3a.rs.{c}", cSC)

                xtile = xlnres[c] if c < 20 else xlnst
                P.wait("vector", cSC, f"p3a.rs.{c}")
                if c == 0:
                    P.wait("vector", cPE, "p2b.done.pe")
                if c >= 20 and c >= 21:
                    P.wait("vector", cGP, f"p3a.xst.{c - 1}")
                VE().tensor_scalar(lnscr, xbf[c % 2], mv2[:, 0:1], stats[:, 6:7],
                                   ALU.subtract, ALU.mult)
                VE().tensor_tensor(lnscr, lnscr, Greg, ALU.mult)
                VE().tensor_tensor(xtile, lnscr, Breg, ALU.add)
                P.inc(vfence(xtile[:, 2047:2048]), cVE, 1)
                P.rec(f"p3a.xcons.{c}", cVE)

                VE().tensor_tensor(lnscr, xtile, xtile, ALU.mult)
                VE().reduce_sum(stats[:, 10:11], lnscr, AX.X)
                P.inc(vfence(stats[:, 10:11]), cVE, 1)
                P.rec(f"p3a.ssq.{c}", cVE)
                P.wait("scalar", cVE, f"p3a.ssq.{c}")
                SC().activation(stats[:, 11:12], stats[:, 10:11], AF.Ln)
                SC().activation(stats[:, 12:13], stats[:, 11:12], AF.Exp, scale=0.5)
                P.inc(sfence(stats[:, 12:13]), cSC, 1)
                P.rec(f"p3a.nr.{c}", cSC)
                P.wait("vector", cSC, f"p3a.nr.{c}")
                VE().tensor_scalar(stats[:, 13:14], stats[:, 12:13], 1e-12, None, ALU.max)
                VE().reciprocal(stats[:, 14:15], stats[:, 13:14])
                VE().tensor_tensor(stats[:, 15:16], stats[:, 14:15], sc_scl[:], ALU.mult)
                VE().tensor_scalar(xnbf, xtile, stats[:, 15:16], None, ALU.mult)
                P.inc(vfence(xnbf[:, 2047:2048]), cVE, 1)
                P.rec(f"p3a.xn.{c}", cVE)

                for dc in range(16):
                    k = c * 16 + dc
                    if dc == 0:
                        P.wait("tensor", cVE, f"p3a.xn.{c}")
                        if k == 0:
                            P.wait("tensor", cSC, "p2.trev.511")
                    if k >= 2:
                        P.wait("tensor", cSC, f"p3a.xnev.{k - 2}")
                    P.inc(TE().transpose(pmT_bf[k % 2][:, 0:128], xnbf[:, dc * 128:(dc + 1) * 128],
                                         ident[:]), cPE, 1)
                    P.rec(f"p3a.xtr.{k}", cPE)
                    P.wait("scalar", cPE, f"p3a.xtr.{k}")
                    if k == 0:
                        P.wait("scalar", cVE, "p3a.musqv.15")
                    P.inc(SC().copy(xnT[dc], pmT_bf[k % 2][:, 0:128]), cSC, 1)
                    P.rec(f"p3a.xnev.{k}", cSC)

                for dc in range(16):
                    if dc == 0:
                        P.wait("tensor", cSC, f"p3a.xnev.{c * 16 + 15}")
                        if c == 0:
                            P.wait("tensor", cVE, "p3a.mun")
                        else:
                            P.wait("tensor", cSC, f"p3a.eexp.{(c - 1) * 4 + 3}")
                    for s4 in range(4):
                        inst = TE().matmul(pmA[s4][:], xnT[dc], mun[dc][:, s4 * 512:(s4 + 1) * 512],
                                           start=(dc == 0), stop=(dc == 15))
                        if s4 == 3 and dc == 15:
                            P.inc(inst, cPE, 1)
                P.rec(f"p3a.lmm.{c * 4 + 3}", cPE)
                for s4 in range(4):
                    k = c * 4 + s4
                    if s4 == 0:
                        P.wait("scalar", cPE, f"p3a.lmm.{c * 4 + 3}")
                    P.inc(SC().activation(esb[:, s4 * 512:(s4 + 1) * 512], pmA[s4][:], AF.Exp,
                                          bias=sc_nab[:], accum_out=stats[:, 17 + s4:18 + s4]), cSC, 1)
                    P.rec(f"p3a.eexp.{k}", cSC)

                P.wait("vector", cSC, f"p3a.eexp.{c * 4 + 3}")
                VE().tensor_tensor(stats[:, 21:22], stats[:, 17:18], stats[:, 18:19], ALU.add)
                VE().tensor_tensor(stats[:, 22:23], stats[:, 19:20], stats[:, 20:21], ALU.add)
                VE().tensor_tensor(stats[:, 16:17], stats[:, 21:22], stats[:, 22:23], ALU.add)
                P.inc(VE().reciprocal(res_all[:, c:c + 1], stats[:, 16:17]), cVE, 1)
                P.rec(f"p3a.res.{c}", cVE)

                P.wait("gpsimd", cVE, f"p3a.xcons.{c}")
                dma(GP, C["XLS"], xln_d[c * 128:(c + 1) * 128, :], xtile)
                P.rec(f"p3a.xst.{c}", C["XLS"])
                P.wait("gpsimd", cVE, f"p3a.res.{c}")
                for ec in range(16):
                    dma(GP, C["EHS"], eh_d[ec, c * 128:(c + 1) * 128, :], esb[:, ec * 128:(ec + 1) * 128])
                P.rec(f"p3a.est.{c}", C["EHS"])

            P.rec("p3a.done.pe", cPE)

        def p3c():
            P.wait("sync", cPE, "p3a.done.pe")
            P.wait("sync", None, "p3a.est.31")
            for c in range(20, 32):
                dma(SY, C["BLK"], xlnres[c], xln_d[c * 128:(c + 1) * 128, :])
            P.rec("p3c.xln", C["BLK"])
            for e in range(NE):
                if e >= 1:
                    P.wait("sync", cPE, f"p3c.simm.{e - 1}")
                ebv = eb3.rearrange("p (c s) -> p c s", s=128)
                esrc = eh_d[e].rearrange("(c p) s -> p c s", p=128)
                for i in range(8):
                    dma(SY, C["EBL"], ebv[:, i * 4:(i + 1) * 4, :], esrc[:, i * 4:(i + 1) * 4, :])
                P.rec(f"p3c.ebld.{e}", C["EBL"])
                if e == 0:
                    P.wait("sync", cGP, "prep.w1")
                    P.wait("sync", cGP, "prep.w2")
                if e >= 1:
                    P.wait("sync", cVE, f"p3c.soev.{e - 1}")
                else:
                    P.wait("sync", cVE, "p3a.xcons.31")
                dma(SY, C["BLD"], B1f, b1_d[e:e + 1, :].broadcast_to((128, HID)))
                dma(SY, C["BLD"], B2f, b2_d[e:e + 1, :].broadcast_to((128, D)))
                P.rec(f"p3c.bld.{e}", C["BLD"])
                for qd in range(4):
                    kq = e * 4 + qd
                    if kq >= 2:
                        P.wait("sync", cPE, f"p3c.hmm.{kq - 2}")
                    w1src = w1b_d[e, qd * 512:(qd + 1) * 512, :].rearrange("(a p) h -> p a h", p=128)
                    dma(SY, cWQL[kq % 2], w1q[kq % 2].rearrange("p (a h) -> p a h", a=4), w1src)
                    P.rec(f"p3c.w1ld.{kq}", cWQL[kq % 2])
                for qd in range(4):
                    kq = e * 4 + qd
                    if kq >= 2:
                        P.wait("sync", cPE, f"p3c.somm.{kq - 2}")
                    dma(SY, cWQL[kq % 2], w2q[kq % 2], w2b_d[e, qd * 128:(qd + 1) * 128, :])
                    P.rec(f"p3c.w2ld.{kq}", cWQL[kq % 2])

                for c in range(NCH):
                    if c == 0:
                        P.wait("tensor", cLD, f"p3c.ebld.{e}")
                        if e == 0:
                            P.wait("tensor", cLD, "p3c.xln")
                            P.wait("tensor", cSC, "p3a.eexp.127")
                        else:
                            P.wait("tensor", cVE, f"p3c.soev.{e - 1}")
                    for s4 in range(4):
                        inst = TE().matmul(pmA[s4][:], eb3[:, c * 128:(c + 1) * 128],
                                           xlnres[c][:, s4 * 512:(s4 + 1) * 512],
                                           start=(c == 0), stop=(c == NCH - 1))
                        if s4 == 3 and c == NCH - 1:
                            P.inc(inst, cPE, 1)
                    TE().matmul(pmB[0][:, 0:1], eb3[:, c * 128:(c + 1) * 128], onesc_bf[:],
                                start=(c == 0), stop=(c == NCH - 1))
                P.rec(f"p3c.simm4.{e * 4 + 3}", cPE)
                for s4 in range(4):
                    k = e * 4 + s4
                    if s4 == 0:
                        P.wait("vector", cPE, f"p3c.simm4.{e * 4 + 3}")
                        if e == 0:
                            P.wait("vector", cSC, "p3a.xnev.511")
                        VE().reciprocal(rn_col[:, 0:1], pmB[0][:, 0:1])
                    P.inc(VE().tensor_scalar(slin[:, s4 * 512:(s4 + 1) * 512], pmA[s4][:],
                                             rn_col[:, 0:1], None, ALU.mult), cVE, 1)
                    P.rec(f"p3c.siev.{k}", cVE)
                P.rec(f"p3c.simm.{e}", cPE)

                for dc in range(16):
                    k = e * 16 + dc
                    if dc == 0:
                        P.wait("tensor", cVE, f"p3c.siev.{e * 4 + 3}")
                        if k == 0:
                            P.wait("tensor", cSC, "p3a.xnev.511")
                    if k >= 2:
                        P.wait("tensor", cSC, f"p3c.siTev.{k - 2}")
                    P.inc(TE().transpose(pmT_bf[k % 2][:, 0:128], slin[:, dc * 128:(dc + 1) * 128],
                                         ident[:]), cPE, 1)
                    P.rec(f"p3c.sitr.{k}", cPE)
                    P.wait("scalar", cPE, f"p3c.sitr.{k}")
                    P.inc(SC().copy(siT[dc], pmT_bf[k % 2][:, 0:128]), cSC, 1)
                    P.rec(f"p3c.siTev.{k}", cSC)

                for dc in range(16):
                    qd = dc // 4
                    kq = e * 4 + qd
                    if dc == 0:
                        P.wait("tensor", cSC, f"p3c.siTev.{e * 16 + 15}")
                        if e >= 1:
                            P.wait("tensor", cVE, f"p3c.hpre.{e - 1}")
                    if dc % 4 == 0:
                        P.wait("tensor", cLD, f"p3c.w1ld.{kq}")
                    inst = TE().matmul(pmB[0][:], siT[dc], w1q[kq % 2][:, (dc % 4) * 512:(dc % 4 + 1) * 512],
                                       start=(dc == 0), stop=(dc == 15))
                    if dc % 4 == 3:
                        P.inc(inst, cPE, 1)
                        P.rec(f"p3c.hmm.{kq}", cPE)
                P.rec(f"p3c.hdone.{e}", cPE)
                P.wait("vector", cPE, f"p3c.hdone.{e}")
                P.wait("vector", None, f"p3c.bld.{e}")
                P.inc(VE().tensor_tensor(hpre, pmB[0][:], B1f, ALU.add), cVE, 1)
                P.rec(f"p3c.hpre.{e}", cVE)
                P.wait("scalar", cVE, f"p3c.hpre.{e}")
                P.inc(SC().activation(hbf, hpre, AF.Gelu), cSC, 1)
                P.rec(f"p3c.gelu.{e}", cSC)

                for i in range(4):
                    k = e * 4 + i
                    if i == 0:
                        P.wait("tensor", cSC, f"p3c.gelu.{e}")
                    if k >= 2:
                        P.wait("tensor", cSC, f"p3c.hTev.{k - 2}")
                    P.inc(TE().transpose(pmT_bf[k % 2][:, 0:128], hbf[:, i * 128:(i + 1) * 128],
                                         ident[:]), cPE, 1)
                    P.rec(f"p3c.htr.{k}", cPE)
                    P.wait("scalar", cPE, f"p3c.htr.{k}")
                    P.inc(SC().copy(hT[i], pmT_bf[k % 2][:, 0:128]), cSC, 1)
                    P.rec(f"p3c.hTev.{k}", cSC)

                for hc in range(4):
                    kq = e * 4 + hc
                    if hc == 0:
                        P.wait("tensor", cSC, f"p3c.hTev.{e * 4 + 3}")
                    P.wait("tensor", cLD, f"p3c.w2ld.{kq}")
                    for s4 in range(4):
                        inst = TE().matmul(pmA[s4][:], hT[hc], w2q[kq % 2][:, s4 * 512:(s4 + 1) * 512],
                                           start=(hc == 0), stop=(hc == 3))
                        if s4 == 3:
                            P.inc(inst, cPE, 1)
                    P.rec(f"p3c.somm.{kq}", cPE)
                P.rec(f"p3c.sodone.{e}", cPE)

                P.wait("vector", cPE, f"p3c.sodone.{e}")
                if e >= 1:
                    P.wait("vector", None, f"p3c.sost.{e - 1}")
                for s4 in range(4):
                    inst = VE().tensor_tensor(sobf[:, s4 * 512:(s4 + 1) * 512], pmA[s4][:],
                                              B2f[:, s4 * 512:(s4 + 1) * 512], ALU.add)
                    if s4 == 3:
                        P.inc(inst, cVE, 1)
                P.rec(f"p3c.soev.{e}", cVE)
                P.wait("gpsimd", cVE, f"p3c.soev.{e}")
                dma(GP, C["SOS"], soh_d[e], sobf)
                P.rec(f"p3c.sost.{e}", C["SOS"])
            P.rec("p3c.done.pe", cPE)

        def p3e():
            P.wait("sync", cPE, "p3c.done.pe")
            P.wait("sync", None, "p3c.sost.15")
            for ec in range(16):
                dma(SY, C["BLK"], sores[ec], soh_d[ec])
            P.rec("p3e.sold", C["BLK"])
            for g4 in range(4):
                if g4 >= 2:
                    P.wait("sync", cPE, f"p3e.gdone.{g4 - 2}")
                for ec in range(16):
                    P.inc(SY().dma_start_transpose(ebT[g4 % 2][ec],
                                                   eh_d[ec, g4 * 1024:(g4 + 1) * 1024, :]), cEBT[g4 % 2], 16)
                P.rec(f"p3e.ebt.{g4}", cEBT[g4 % 2])
                for cc in range(8):
                    c = g4 * 8 + cc
                    if c < 2:
                        P.wait("sync", cVE, f"p3a.xcons.{30 + c}")
                        P.wait("sync", cVE, "p3c.soev.15")
                    else:
                        P.wait("sync", cVE, f"p3e.sev.{(c - 2) * 4 + 3}")
                    dma(SY, cX[c % 2], xbf[c % 2], x1_d[c * 128:(c + 1) * 128, :])
                    P.rec(f"p3e.xld.{c}", cX[c % 2])
                    for ec in range(16):
                        if ec == 0:
                            P.wait("tensor", cLD, f"p3e.xld.{c}")
                            P.wait("tensor", cLD, f"p3e.ebt.{g4}")
                            if c == 0:
                                P.wait("tensor", cLD, "p3e.sold")
                                P.wait("tensor", cVE, "p3c.soev.15")
                            else:
                                P.wait("tensor", cVE, f"p3e.sev.{(c - 1) * 4 + 3}")
                        for s4 in range(4):
                            inst = TE().matmul(pmA[s4][:], ebT[g4 % 2][ec][:, cc * 128:(cc + 1) * 128],
                                               sores[ec][:, s4 * 512:(s4 + 1) * 512],
                                               start=(ec == 0), stop=(ec == 15))
                            if s4 == 3 and ec == 15:
                                P.inc(inst, cPE, 1)
                    P.rec(f"p3e.mm.{c * 4 + 3}", cPE)
                    for s4 in range(4):
                        k = c * 4 + s4
                        if s4 == 0:
                            P.wait("vector", cPE, f"p3e.mm.{c * 4 + 3}")
                            if c >= 2:
                                P.wait("vector", None, f"p3e.ost.{c - 2}")
                        P.inc(VE().scalar_tensor_tensor(osb[c % 2][:, s4 * 512:(s4 + 1) * 512],
                                                        pmA[s4][:], res_all[:, c:c + 1],
                                                        xbf[c % 2][:, s4 * 512:(s4 + 1) * 512],
                                                        ALU.mult, ALU.add), cVE, 1)
                        P.rec(f"p3e.sev.{k}", cVE)
                    P.wait("gpsimd", cVE, f"p3e.sev.{c * 4 + 3}")
                    dma(GP, cOUT[c % 2], out_d[c * 128:(c + 1) * 128, :], osb[c % 2])
                    P.rec(f"p3e.ost.{c}", cOUT[c % 2])
                P.rec(f"p3e.gdone.{g4}", cPE)
            P.wait("gpsimd", None, "p3e.ost.30")
            P.wait("gpsimd", None, "p3e.ost.31")

        def program():
            phases = (("p0", p0_prep), ("p1", p1), ("p2", p2), ("p2b", p2b),
                      ("p3a", p3a), ("p3c", p3c), ("p3e", p3e))
            for name, ph in phases[:upto]:
                with nc.named_scope(name):
                    ph()

        P.begin("count")
        program()

        with nc.Block() as blk0:
            @blk0.sync
            def _(eng):
                for s in sems.values():
                    nc.sync.sem_clear(s)

        with nc.Block() as block:
            @block.sync
            def _(eng):
                P.begin("sync")
                program()

            @block.gpsimd
            def _(eng):
                P.begin("gpsimd")
                program()

            @block.tensor
            def _(eng):
                P.begin("tensor")
                program()

            @block.vector
            def _(eng):
                P.begin("vector")
                program()

            @block.scalar
            def _(eng):
                P.begin("scalar")
                program()

    return nc


def kernel(**inputs):
    x = np.asarray(inputs["L0_attn_x"], dtype=np.float32)
    C, S = _rope_tables()
    common = {
        "wq": np.ascontiguousarray(np.asarray(inputs["L0_attn_Wq"], dtype=np.float32)),
        "wk": np.ascontiguousarray(np.asarray(inputs["L0_attn_Wk"], dtype=np.float32)),
        "wv": np.ascontiguousarray(np.asarray(inputs["L0_attn_Wv"], dtype=np.float32)),
        "wo": np.ascontiguousarray(np.asarray(inputs["L0_attn_Wo"], dtype=np.float32)),
        "gam": np.ascontiguousarray(np.asarray(inputs["L0_ffn_gamma"], dtype=np.float32)),
        "bet": np.ascontiguousarray(np.asarray(inputs["L0_ffn_beta"], dtype=np.float32)),
        "mu": np.ascontiguousarray(np.asarray(inputs["L0_ffn_mu"], dtype=np.float32).reshape(D, NE * NS)),
        "scl": np.asarray(inputs["L0_ffn_scale"], dtype=np.float32).reshape(1, 1),
        "w1": np.ascontiguousarray(np.asarray(inputs["L0_ffn_W1"], dtype=np.float32)),
        "b1": np.ascontiguousarray(np.asarray(inputs["L0_ffn_b1"], dtype=np.float32)),
        "w2": np.ascontiguousarray(np.asarray(inputs["L0_ffn_W2"], dtype=np.float32)),
        "b2": np.ascontiguousarray(np.asarray(inputs["L0_ffn_b2"], dtype=np.float32)),
        "ident": np.eye(128, dtype=np.float32).astype(BF),
        "ropec": C, "ropes": S, "mask": _mask_tiles(),
    }
    nc = build()
    in_maps = [dict(common, x=np.ascontiguousarray(x[b])) for b in range(8)]
    res = run_bass_kernel_spmd(nc, in_maps, core_ids=list(range(8)))
    return np.stack([np.asarray(r["out"], dtype=np.float32) for r in res.results], axis=0)


# version 06 (final)
# speedup vs baseline: 1.0925x; 1.0925x over previous
import contextlib
import math
import numpy as np
import ml_dtypes

import concourse.bass as bass
import concourse.mybir as mybir
from concourse.bass_utils import run_bass_kernel_spmd

F32 = mybir.dt.float32
BF16 = mybir.dt.bfloat16
AF = mybir.ActivationFunctionType
ALU = mybir.AluOpType
AX = mybir.AxisListType
BF = ml_dtypes.bfloat16

NT, D, NH, DH, NE, NS, HID = 4096, 2048, 16, 128, 16, 128, 512
NCH = 32
EPS = 1e-5
ENGS = ("sync", "gpsimd", "tensor", "vector", "scalar")
ISQ = float(1.0 / math.sqrt(DH))


def _rope_tables():
    inv_freq = (1.0 / np.power(np.float32(10000.0),
                               (np.arange(0, DH, 2, dtype=np.float32) / np.float32(DH)))).astype(np.float32)
    t = np.arange(NT, dtype=np.float32)
    freqs = t[:, None] * inv_freq[None, :]
    emb = np.concatenate([freqs, freqs], axis=1)
    cos_h = np.cos(emb)[:, ::2].astype(np.float32)
    sin_h = np.sin(emb)[:, ::2].astype(np.float32)
    idx = np.tile(np.arange(64), 2)
    return cos_h[:, idx].T.copy().astype(BF), sin_h[:, idx].T.copy().astype(BF)


def _mask_tiles():
    m = np.zeros((128, 4 * 512), dtype=np.float32)
    i = np.arange(128)[:, None]
    q = np.arange(512)[None, :]
    for rel in range(4):
        m[:, rel * 512:(rel + 1) * 512] = (rel * 128 + i <= q).astype(np.float32)
    return m.astype(BF)


class _Ctr:
    __slots__ = ("name", "v", "sem")

    def __init__(self, name):
        self.name, self.v, self.sem = name, 0, None


class _DummyInst:
    def then_inc(self, *a, **k):
        return self


_DINST = _DummyInst()


class _DummyEng:
    def __getattr__(self, name):
        def f(*a, **k):
            return _DINST
        return f


_DENG = _DummyEng()


class _VecHazardWrap:

    _OUTK = ("accum_out",)

    def __init__(self, eng, sem):
        self._e = eng
        self._sem = sem
        self._n = 0
        self._lr = []
        self._lw = []
        self._li = None
        self._li_sync = None

    @staticmethod
    def _iv(ap):
        try:
            name = ap.tensor.name
            lo = int(ap.offset)
            hi = lo
            for step, count in ap.ap:
                hi += int(step) * (int(count) - 1)
            return (name, min(lo, hi), max(lo, hi) + 1)
        except Exception:
            return None

    def __getattr__(self, fname):
        f = getattr(self._e, fname)
        if fname in ("wait_ge", "engine_nop", "nop", "drain") or not callable(f):
            return f
        wrap_self = self

        def call(*args, **kwargs):
            outs, ins = [], []
            for i, a in enumerate(args):
                if isinstance(a, bass.AP):
                    (outs if i == 0 else ins).append(a)
            for k, v in kwargs.items():
                if isinstance(v, bass.AP):
                    (outs if k in wrap_self._OUTK else ins).append(v)
            ivo = [x for x in map(wrap_self._iv, outs) if x]
            ivi = [x for x in map(wrap_self._iv, ins) if x]

            def clash(a, b):
                return any(x[0] == y[0] and x[1] < y[2] and y[1] < x[2]
                           for x in a for y in b)

            if (clash(wrap_self._lw, ivi) or clash(wrap_self._lw, ivo)
                    or clash(wrap_self._lr, ivo)):
                if wrap_self._li_sync is not None:
                    wrap_self._e.wait_ge(*wrap_self._li_sync)
                elif wrap_self._li is not None:
                    wrap_self._n += 1
                    wrap_self._li.then_inc(wrap_self._sem, 1)
                    wrap_self._e.wait_ge(wrap_self._sem, wrap_self._n)
                wrap_self._lr, wrap_self._lw = [], []
            inst = f(*args, **kwargs)
            wrap_self._lr, wrap_self._lw = ivi, ivo
            wrap_self._li = inst
            wrap_self._li_sync = None
            return inst

        return call


class Env:
    def __init__(self, real_engs):
        self.real = real_engs
        self.mode = "count"
        self.ev = {}
        self.e = {}
        self.ctrs = {}

    def ctr(self, name):
        c = _Ctr(name)
        self.ctrs[name] = c
        return c

    def begin(self, mode):
        self.mode = mode
        for c in self.ctrs.values():
            c.v = 0
        for n in ENGS:
            if mode == n:
                if n == "vector":
                    self.e[n] = _VecHazardWrap(self.real[n], self.vself_sem)
                else:
                    self.e[n] = self.real[n]
            else:
                self.e[n] = _DENG

    def inc(self, inst, ctr, k=1):
        ctr.v += k
        if inst is not _DINST:
            inst.then_inc(ctr.sem, k)
            w = self.e.get("vector")
            if isinstance(w, _VecHazardWrap) and w._li is inst:
                w._li_sync = (ctr.sem, ctr.v)
        return ctr.v

    def rec(self, key, ctr):
        if self.mode == "count":
            self.ev[key] = (ctr, ctr.v)
        else:
            c0, v0 = self.ev[key]
            assert c0 is ctr and v0 == ctr.v, (key, c0.name, v0, ctr.name, ctr.v)

    def wait(self, eng, _ctr_ignored, key=None):
        if key is None:
            key = _ctr_ignored
        if self.mode == "count":
            return
        ctr, v = self.ev[key]
        if v > 0 and self.e[eng] is not _DENG:
            self.e[eng].wait_ge(ctr.sem, v)


def build(dbg=False, upto=7):
    nc = bass.Bass("TRN2", target_bir_lowering=False, detect_race_conditions=False)
    skind = "ExternalOutput" if dbg else "Internal"

    x_d = nc.dram_tensor("x", [NT, D], F32, kind="ExternalInput").ap()
    wq_d = nc.dram_tensor("wq", [D, D], F32, kind="ExternalInput").ap()
    wk_d = nc.dram_tensor("wk", [D, D], F32, kind="ExternalInput").ap()
    wv_d = nc.dram_tensor("wv", [D, D], F32, kind="ExternalInput").ap()
    wo_d = nc.dram_tensor("wo", [D, D], F32, kind="ExternalInput").ap()
    gam_d = nc.dram_tensor("gam", [D], F32, kind="ExternalInput").ap()
    bet_d = nc.dram_tensor("bet", [D], F32, kind="ExternalInput").ap()
    mu_d = nc.dram_tensor("mu", [D, NE * NS], F32, kind="ExternalInput").ap()
    scl_d = nc.dram_tensor("scl", [1, 1], F32, kind="ExternalInput").ap()
    w1_d = nc.dram_tensor("w1", [NE, D, HID], F32, kind="ExternalInput").ap()
    b1_d = nc.dram_tensor("b1", [NE, HID], F32, kind="ExternalInput").ap()
    w2_d = nc.dram_tensor("w2", [NE, HID, D], F32, kind="ExternalInput").ap()
    b2_d = nc.dram_tensor("b2", [NE, D], F32, kind="ExternalInput").ap()
    ident_d = nc.dram_tensor("ident", [128, 128], BF16, kind="ExternalInput").ap()
    ropec_d = nc.dram_tensor("ropec", [128, NT], BF16, kind="ExternalInput").ap()
    ropes_d = nc.dram_tensor("ropes", [128, NT], BF16, kind="ExternalInput").ap()
    mask_d = nc.dram_tensor("mask", [128, 2048], BF16, kind="ExternalInput").ap()

    out_d = nc.dram_tensor("out", [NT, D], F32, kind="ExternalOutput").ap()

    wqp_d = nc.dram_tensor("wqp", [D, D], BF16, kind=skind).ap()
    wkp_d = nc.dram_tensor("wkp", [D, D], BF16, kind="Internal").ap()
    wvb_d = nc.dram_tensor("wvb", [D, D], BF16, kind="Internal").ap()
    wob_d = nc.dram_tensor("wob", [D, D], BF16, kind="Internal").ap()
    w1b_d = nc.dram_tensor("w1b", [NE, D, HID], BF16, kind="Internal").ap()
    w2b_d = nc.dram_tensor("w2b", [NE, HID, D], BF16, kind="Internal").ap()
    qt_d = nc.dram_tensor("qt", [NH, DH, NT], BF16, kind=skind).ap()
    kt_d = nc.dram_tensor("kt", [NH, DH, NT], BF16, kind=skind).ap()
    vb_d = nc.dram_tensor("vb", [NH, NT, DH], BF16, kind=skind).ap()
    atn_d = nc.dram_tensor("atn", [NCH, NH, DH, 128], BF16, kind=skind).ap()
    x1_d = nc.dram_tensor("x1h", [NT, D], F32, kind=skind).ap()
    xln_d = nc.dram_tensor("xlnh", [NT, D], BF16, kind=skind).ap()
    eh_d = nc.dram_tensor("eh", [NE, NT, NS], BF16, kind=skind).ap()
    soh_d = nc.dram_tensor("soh", [NE, NS, D], BF16, kind=skind).ap()
    rnh_d = nc.dram_tensor("rnh", [1, 2048], F32, kind="Internal").ap()

    with contextlib.ExitStack() as ctx:
        sb_ = lambda *a: ctx.enter_context(nc.sbuf_tensor(*a))
        ps_ = lambda *a: ctx.enter_context(nc.psum_tensor(*a))

        AR = sb_("AR", [128, 81920], BF16)
        ARF = sb_("ARF", [128, 10240], F32)
        ident = sb_("identt", [128, 128], BF16)
        stats = sb_("stats", [128, 32], F32)
        bn6 = sb_("bn6", [128, 24], F32)
        mv2 = sb_("mv2", [128, 4], F32)
        res_all = sb_("res_all", [128, 32], F32)
        rn_col = sb_("rn_col", [128, 16], F32)
        junkbf = sb_("junkbf", [128, 2048], BF16)
        rl_sb = sb_("rl_sb", [128, 4], F32)
        ones1_bf = sb_("ones1_bf", [1, 128], BF16)
        onesc_bf = sb_("onesc_bf", [128, 1], BF16)
        ones1_f = sb_("ones1_f", [1, 128], F32)
        onesc_f = sb_("onesc_f", [128, 1], F32)
        sc_scl = sb_("sc_scl", [128, 1], F32)
        sc_nab = sb_("sc_nab", [128, 1], F32)
        s_sc = sb_("s_sc", [1, 2], F32)
        eps128 = sb_("eps128", [128, 1], F32)
        eps2_128 = sb_("eps2_128", [128, 1], F32)
        hpre = None

        pmA = [ps_(f"pmA{i}", [128, 512], F32) for i in range(4)]
        pmB = [ps_(f"pmB{i}", [128, 512], F32) for i in range(2)]
        pmT = [ps_(f"pmT{i}", [128, 512], F32) for i in range(2)]
        pmT_bf = [p[:].bitcast(BF16) for p in pmT]

        SEM_NAMES = ("PE", "SC", "VE",
                     "CON", "WTL", "X0", "X1", "HB0", "HB1", "AB0", "AB1", "EBL",
                     "WQL0", "WQL1", "EBT0", "EBT1", "BLK",
                     "PQ", "PK", "PV", "PO", "PW1", "PW2", "MUB", "BLD",
                     "ROT0", "ROT1", "LNB0", "LNB1", "ATS0", "ATS1",
                     "X1S0", "X1S1", "XLS", "EHS", "SOS", "OUT0", "OUT1", "VSELF")
        sems = {n: ctx.enter_context(nc.semaphore(n)) for n in SEM_NAMES}
        real = {"sync": nc.sync, "gpsimd": nc.gpsimd, "tensor": nc.tensor,
                "vector": nc.vector, "scalar": nc.scalar}
        P = Env(real)
        C = {}
        for n in SEM_NAMES:
            C[n] = P.ctr(n)
            C[n].sem = sems[n]
        cPE, cSC, cVE = C["PE"], C["SC"], C["VE"]
        P.vself_sem = sems["VSELF"]
        cLD = cGP = None
        cX = [C["X0"], C["X1"]]
        cHB = [C["HB0"], C["HB1"]]
        cAB = [C["AB0"], C["AB1"]]
        cWQL = [C["WQL0"], C["WQL1"]]
        cEBT = [C["EBT0"], C["EBT1"]]
        cROT = [C["ROT0"], C["ROT1"]]
        cLNB = [C["LNB0"], C["LNB1"]]
        cATS = [C["ATS0"], C["ATS1"]]
        cX1S = [C["X1S0"], C["X1S1"]]
        cOUT = [C["OUT0"], C["OUT1"]]

        def arr(lo, n):
            return AR[:, lo:lo + n]

        def arf(lo, n):
            return ARF[:, lo:lo + n]

        lnT = [arr(dc * 2048, 2048) for dc in range(16)]
        WT = [arr(32768 + dc * 2048, 2048) for dc in range(16)]
        lnbf = [arr(65536 + i * 2048, 2048) for i in range(2)]
        qeqo = [arr(69632 + i * 1024, 1024) for i in range(2)]
        rot = [arr(71680 + i * 1024, 1024) for i in range(2)]
        ropeC = arr(73728, 4096)
        ropeS = arr(77824, 4096)
        masks = arr(0, 2048)
        Qb = [arr(2048 + i * 4096, 4096) for i in range(2)]
        Kb = [arr(10240 + i * 4096, 4096) for i in range(2)]
        Vaug = [arr(18432 + i * 4128, 4128) for i in range(2)]
        Pt = [arr(26688 + i * 512, 512) for i in range(4)]
        accbf = [arr(28736 + i * 128, 128) for i in range(4)]
        atst = [arr(29248 + i * 128, 128) for i in range(8)]
        woT = WT
        ab2 = [arr(i * 2048, 2048) for i in range(2)]
        xlnres = [arr(c * 2048, 2048) for c in range(32)]
        must = [arr(40960 + i * 2048, 2048) for i in range(2)]
        esb = arr(40960, 2048)
        xnT = [arr(43008 + dc * 128, 128) for dc in range(16)]
        xlnst = arr(45056, 2048)
        xnbf = arr(47104, 2048)
        mun = [arr(49152 + dc * 2048, 2048) for dc in range(16)]
        eb3 = arr(65536, 4096)
        w1q = [arr(69632 + i * 2048, 2048) for i in range(2)]
        w2q = [arr(73728 + i * 2048, 2048) for i in range(2)]
        slin = arr(77824, 2048)
        hbf = arr(77824, 512)
        sobf = arr(77824, 2048)
        siT = [arr(79872 + dc * 128, 128) for dc in range(16)]
        hT = [arr(79872 + i * 128, 128) for i in range(4)]
        sores = [arr(ec * 2048, 2048) for ec in range(16)]
        ebT = [[arr(32768 + b * 16384 + ec * 1024, 1024) for ec in range(16)] for b in range(2)]
        xbf = [arf(i * 2048, 2048) for i in range(2)]
        sqscr = arf(4096, 2048)
        Greg = arf(4096, 2048)
        lnscr = arf(6144, 2048)
        Breg = arf(8192, 2048)
        rnb = arf(0, 2048)
        rnrow = arf(6144, 2048)
        osb = [arf(4096 + i * 2048, 2048) for i in range(2)]
        hpre = arf(0, 512)
        B1f = arf(512, 512)
        B2f = arf(1024, 2048)

        def SY():
            return P.e["sync"]

        def GP():
            return P.e["gpsimd"]

        def TE():
            return P.e["tensor"]

        def VE():
            return P.e["vector"]

        def SC():
            return P.e["scalar"]

        def dma(engf, ctr, out, in_, **kw):
            return P.inc(engf().dma_start(out, in_, **kw), ctr, 16)

        def vfence(src):
            return VE().tensor_copy(stats[:, 31:32], src)

        def sfence(src):
            return SC().copy(stats[:, 30:31], src)

        def vmargin():
            for _ in range(4):
                VE().engine_nop()

        def p0_prep():
            wq5 = wq_d.rearrange("(p hh j t) d -> p t hh j d", p=8, hh=2, j=64, t=2)
            wk5 = wk_d.rearrange("(p hh j t) d -> p t hh j d", p=8, hh=2, j=64, t=2)
            wqp5 = wqp_d.rearrange("(p t hh j) d -> p t hh j d", p=8, t=2, hh=2, j=64)
            wkp5 = wkp_d.rearrange("(p t hh j) d -> p t hh j d", p=8, t=2, hh=2, j=64)
            for i in range(8):
                dma(GP, C["PQ"], wqp5[i], wq5[i])
            P.rec("prep.wq", C["PQ"])
            P.wait("gpsimd", None, "prep.wq")
            for i in range(8):
                dma(GP, C["PK"], wkp5[i], wk5[i])
            P.rec("prep.wk", C["PK"])
            P.wait("gpsimd", None, "prep.wk")
            for i in range(4):
                dma(GP, C["PV"], wvb_d[i * 512:(i + 1) * 512, :], wv_d[i * 512:(i + 1) * 512, :])
            P.rec("prep.wv", C["PV"])
            P.wait("gpsimd", None, "prep.wv")
            for i in range(4):
                dma(GP, C["PO"], wob_d[i * 512:(i + 1) * 512, :], wo_d[i * 512:(i + 1) * 512, :])
            P.rec("prep.wo", C["PO"])
            P.wait("gpsimd", None, "prep.wo")
            for e in range(NE):
                dma(GP, C["PW1"], w1b_d[e], w1_d[e])
            P.rec("prep.w1", C["PW1"])
            for e in range(NE):
                dma(GP, C["PW2"], w2b_d[e], w2_d[e])
            P.rec("prep.w2", C["PW2"])

            dma(SY, C["CON"], ident[:], ident_d)
            dma(SY, C["CON"], ropeC, ropec_d)
            dma(SY, C["CON"], ropeS, ropes_d)
            dma(SY, C["CON"], s_sc[:, 0:1], scl_d)
            P.rec("p0.consts", C["CON"])

            VE().memset(ones1_bf[:], 1.0)
            VE().memset(onesc_bf[:], 1.0)
            VE().memset(ones1_f[:], 1.0)
            VE().memset(eps128[:], float(EPS))
            VE().memset(eps2_128[:], float(EPS * EPS))
            P.inc(VE().memset(onesc_f[:], 1.0), cVE, 1)
            P.rec("p0.ones", cVE)

            P.wait("scalar", cLD, "p0.consts")
            P.inc(SC().activation(s_sc[:, 1:2], s_sc[:, 0:1], AF.Abs), cSC, 1)
            P.rec("p0.abs", cSC)
            P.wait("tensor", cSC, "p0.abs")
            P.wait("tensor", cVE, "p0.ones")
            TE().matmul(pmA[0][:, 0:1], ones1_f[:], s_sc[:, 1:2], start=True, stop=True)
            P.inc(TE().matmul(pmA[0][:, 1:2], ones1_f[:], s_sc[:, 0:1], start=True, stop=True), cPE, 1)
            P.rec("p0.sclmm", cPE)
            P.wait("vector", cPE, "p0.sclmm")
            VE().tensor_scalar(sc_nab[:], pmA[0][:, 0:1], -1.0, None, ALU.mult)
            VE().tensor_copy(sc_scl[:], pmA[0][:, 1:2])
            P.inc(vfence(sc_scl[:]), cVE, 1)
            P.rec("p0.scl", cVE)

        def p1():
            for half in range(2):
                tok0 = half * 2048
                for c in range(16):
                    gc = half * 16 + c
                    if gc >= 2:
                        P.wait("sync", cVE, f"p1.xcons.{gc - 2}")
                    dma(SY, cX[gc % 2], xbf[c % 2], x_d[gc * 128:(gc + 1) * 128, :])
                    P.rec(f"p1.xld.{gc}", cX[gc % 2])

                    P.wait("vector", cLD, f"p1.xld.{gc}")
                    if gc == 0:
                        P.wait("vector", None, "p1.xld.1")
                    if gc >= 2:
                        P.wait("vector", cSC, f"p1.rs.{gc - 2}")
                    if gc == 16:
                        P.wait("vector", None, "p1.vst.0.14")
                    if gc == 17:
                        P.wait("vector", None, "p1.vst.0.15")
                    vmargin()
                    VE().reduce_sum(mv2[:, 0:1], xbf[c % 2], AX.X)
                    VE().tensor_tensor(lnscr, xbf[c % 2], xbf[c % 2], ALU.mult)
                    VE().reduce_sum(mv2[:, 1:2], lnscr, AX.X)
                    VE().tensor_scalar(mv2[:, 0:1], mv2[:, 0:1], float(1.0 / D), None, ALU.mult)
                    VE().tensor_scalar(mv2[:, 1:2], mv2[:, 1:2], float(1.0 / D), None, ALU.mult)
                    VE().tensor_tensor(stats[:, 7:8], mv2[:, 0:1], mv2[:, 0:1], ALU.mult)
                    VE().tensor_tensor(mv2[:, 1:2], mv2[:, 1:2], stats[:, 7:8], ALU.subtract)
                    P.inc(vfence(mv2[:, 1:2]), cVE, 1)
                    P.rec(f"p1.mv.{gc}", cVE)

                    P.wait("scalar", cVE, f"p1.mv.{gc}")
                    if gc == 16:
                        P.wait("scalar", cPE, "p1.passend.0.v")
                    SC().activation(stats[:, 0:1], mv2[:, 1:2], AF.Ln, bias=eps128[:])
                    SC().activation(stats[:, 1:2], stats[:, 0:1], AF.Exp, scale=-0.5)
                    P.inc(sfence(stats[:, 1:2]), cSC, 1)
                    P.rec(f"p1.rs.{gc}", cSC)

                    P.wait("vector", cSC, f"p1.rs.{gc}")
                    if gc >= 2:
                        P.wait("vector", cPE, f"p1.tr16.{gc - 2}")
                    VE().tensor_scalar(lnbf[c % 2], xbf[c % 2], mv2[:, 0:1],
                                       stats[:, 1:2], ALU.subtract, ALU.mult)
                    P.inc(vfence(lnbf[c % 2][:, 2047:2048]), cVE, 1)
                    P.rec(f"p1.xcons.{gc}", cVE)

                    P.wait("tensor", cVE, f"p1.xcons.{gc}")
                    if gc == 0:
                        P.wait("tensor", cLD, "p0.consts")
                    for dc in range(16):
                        k = gc * 16 + dc
                        if k >= 2:
                            P.wait("tensor", cSC, f"p1.lnev.{k - 2}")
                        P.inc(TE().transpose(pmT_bf[k % 2][:, 0:128],
                                             lnbf[c % 2][:, dc * 128:(dc + 1) * 128], ident[:]), cPE, 1)
                        P.rec(f"p1.tr.{k}", cPE)
                        P.wait("scalar", cPE, f"p1.tr.{k}")
                        P.inc(SC().copy(lnT[dc][:, c * 128:(c + 1) * 128], pmT_bf[k % 2][:, 0:128]), cSC, 1)
                        P.rec(f"p1.lnev.{k}", cSC)
                    P.rec(f"p1.tr16.{gc}", cPE)
                P.rec(f"p1.lnall.{half}", cSC)

                for wname, wsrc, dst in (("q", wqp_d, qt_d), ("k", wkp_d, kt_d)):
                    P.wait("sync", cGP, f"prep.w{wname}")
                    prev = {("q", 0): None, ("k", 0): "p1.passend.0.q",
                            ("q", 1): "p1.passend.0.v", ("k", 1): "p1.passend.1.q"}[(wname, half)]
                    if prev:
                        P.wait("sync", cPE, prev)
                    for dc in range(16):
                        P.inc(SY().dma_start_transpose(WT[dc], wsrc[:, dc * 128:(dc + 1) * 128]), C["WTL"], 16)
                    P.rec(f"p1.wt.{half}.{wname}", C["WTL"])
                    for g in range(4):
                        for a in range(8):
                            u = g * 8 + a
                            pq = (pmB[0], pmB[1]) if a % 2 == 0 else (pmA[0], pmA[1])
                            if u == 0:
                                P.wait("tensor", cLD, f"p1.wt.{half}.{wname}")
                                P.wait("tensor", cSC, f"p1.lnall.{half}")
                                if half == 0 and wname == "q":
                                    P.wait("tensor", cVE, "p0.scl")
                            if u >= 2:
                                P.wait("tensor", cSC, f"p1.ev.{wname}.{half}.{u - 2}")
                            for dc in range(16):
                                inst = TE().matmul(pq[0][:], WT[dc][:, (2 * a) * 128:(2 * a + 1) * 128],
                                                   lnT[dc][:, g * 512:(g + 1) * 512],
                                                   start=(dc == 0), stop=(dc == 15))
                                if dc == 15:
                                    P.inc(inst, cPE, 1)
                            P.rec(f"p1.mm0.{wname}.{half}.{u}", cPE)
                            for dc in range(16):
                                inst = TE().matmul(pq[1][:], WT[dc][:, (2 * a + 1) * 128:(2 * a + 2) * 128],
                                                   lnT[dc][:, g * 512:(g + 1) * 512],
                                                   start=(dc == 0), stop=(dc == 15))
                                if dc == 15:
                                    P.inc(inst, cPE, 1)
                            P.rec(f"p1.mm1.{wname}.{half}.{u}", cPE)

                            P.wait("scalar", cPE, f"p1.mm1.{wname}.{half}.{u}")
                            if u >= 2:
                                P.wait("scalar", cVE, f"p1.rope.{wname}.{half}.{u - 2}")
                            SC().copy(qeqo[a % 2][:, 0:512], pq[0][:])
                            P.inc(SC().copy(qeqo[a % 2][:, 512:1024], pq[1][:]), cSC, 1)
                            P.rec(f"p1.ev.{wname}.{half}.{u}", cSC)

                            P.wait("vector", cSC, f"p1.ev.{wname}.{half}.{u}")
                            if u >= 2:
                                P.wait("vector", cGP, f"p1.st.{wname}.{half}.{u - 2}")
                            if half == 0 and wname == "q" and u == 0:
                                P.wait("vector", cLD, "p0.consts")
                            ts = slice(tok0 + g * 512, tok0 + (g + 1) * 512)
                            qe = qeqo[a % 2][:, 0:512]
                            qo = qeqo[a % 2][:, 512:1024]
                            t1 = lnbf[0][:, 0:512]
                            t2 = lnbf[0][:, 512:1024]
                            VE().tensor_tensor(t1, qe, ropeC[:, ts], ALU.mult)
                            VE().tensor_tensor(t2, qo, ropeS[:, ts], ALU.mult)
                            VE().tensor_tensor(rot[a % 2][:, 0:512], t1, t2, ALU.subtract)
                            VE().tensor_tensor(t1, qo, ropeC[:, ts], ALU.mult)
                            VE().tensor_tensor(t2, qe, ropeS[:, ts], ALU.mult)
                            P.inc(VE().tensor_tensor(rot[a % 2][:, 512:1024], t1, t2, ALU.add), cVE, 1)
                            P.rec(f"p1.rope.{wname}.{half}.{u}", cVE)

                            P.wait("gpsimd", cVE, f"p1.rope.{wname}.{half}.{u}")
                            dma(GP, cROT[u % 2], dst[2 * a, 0:64, ts], rot[a % 2][0:64, 0:512])
                            dma(GP, cROT[u % 2], dst[2 * a + 1, 0:64, ts], rot[a % 2][64:128, 0:512])
                            dma(GP, cROT[u % 2], dst[2 * a, 64:128, ts], rot[a % 2][0:64, 512:1024])
                            dma(GP, cROT[u % 2], dst[2 * a + 1, 64:128, ts], rot[a % 2][64:128, 512:1024])
                            P.rec(f"p1.st.{wname}.{half}.{u}", cROT[u % 2])
                    P.rec(f"p1.passend.{half}.{wname}", cPE)

                P.wait("sync", cGP, "prep.wv")
                P.wait("sync", cPE, f"p1.passend.{half}.k")
                for dc in range(16):
                    P.inc(SY().dma_start_transpose(WT[dc], wvb_d[:, dc * 128:(dc + 1) * 128]), C["WTL"], 16)
                P.rec(f"p1.wt.{half}.v", C["WTL"])
                for cc in range(16):
                    gc = half * 16 + cc
                    for s4 in range(4):
                        k = cc * 4 + s4
                        if k == 0:
                            P.wait("tensor", cLD, f"p1.wt.{half}.v")
                        if k >= 2:
                            P.wait("tensor", cSC, f"p1.vev.{half}.{k - 2}")
                        for dc in range(16):
                            inst = TE().matmul(pmB[k % 2][:], lnT[dc][:, cc * 128:(cc + 1) * 128],
                                               WT[dc][:, s4 * 512:(s4 + 1) * 512],
                                               start=(dc == 0), stop=(dc == 15))
                            if dc == 15:
                                P.inc(inst, cPE, 1)
                        P.rec(f"p1.vmm.{half}.{k}", cPE)
                        P.wait("scalar", cPE, f"p1.vmm.{half}.{k}")
                        if s4 == 0 and cc == 0:
                            P.wait("scalar", cVE, f"p1.rope.k.{half}.31")
                        if s4 == 0 and cc >= 2:
                            P.wait("scalar", None, f"p1.vst.{half}.{cc - 2}")
                        P.inc(SC().copy(lnbf[cc % 2][:, s4 * 512:(s4 + 1) * 512], pmB[k % 2][:]), cSC, 1)
                        P.rec(f"p1.vev.{half}.{k}", cSC)
                    P.wait("gpsimd", cSC, f"p1.vev.{half}.{cc * 4 + 3}")
                    for h in range(NH):
                        dma(GP, cLNB[cc % 2], vb_d[h, gc * 128:(gc + 1) * 128, :],
                            lnbf[cc % 2][:, h * 128:(h + 1) * 128])
                    P.rec(f"p1.vst.{half}.{cc}", cLNB[cc % 2])
                P.rec(f"p1.passend.{half}.v", cPE)
            P.rec("p1.done.pe", cPE)
            P.rec("p1.done.gp", cROT[1])

        def p2():
            P.wait("sync", cPE, "p1.done.pe")
            dma(SY, C["CON"], masks, mask_d)
            P.rec("p2.mask", C["CON"])
            P.wait("vector", cPE, "p1.done.pe")
            for hb in range(2):
                va = Vaug[hb].rearrange("p (c w) -> p c w", w=129)
                P.inc(VE().memset(va[:, :, 128:129], 1.0), cVE, 1)
            P.rec("p2.vones", cVE)

            it = 0
            pending_endgame = [None]
            for h in range(NH):
                hb = h % 2
                if h >= 2:
                    P.wait("sync", cPE, f"p2.hdone.{h - 2}")
                else:
                    P.wait("sync", None, "p1.st.k.1.31")
                    P.wait("sync", None, "p1.st.k.1.30")
                    P.wait("sync", None, "p1.vst.1.15")
                    P.wait("sync", None, "p1.vst.1.14")
                    P.wait("sync", cVE, "p2.vones")
                dma(SY, cHB[hb], Qb[hb], qt_d[h])
                dma(SY, cHB[hb], Kb[hb], kt_d[h])
                va = Vaug[hb].rearrange("p (c w) -> p c w", w=129)
                vsrc = vb_d[h].rearrange("(c p) e -> p c e", p=128)
                for i in range(8):
                    dma(SY, cHB[hb], va[:, i * 4:(i + 1) * 4, 0:128], vsrc[:, i * 4:(i + 1) * 4, :])
                P.rec(f"p2.hld.{h}", cHB[hb])

                for qb in range(8):
                    kmax = 4 * qb + 4
                    gqb = h * 8 + qb
                    it0 = it

                    def emit_S(itx, kkx):
                        six = itx % 2
                        if kkx == 0:
                            P.wait("tensor", cLD, f"p2.hld.{h}")
                            if itx == 0:
                                P.wait("tensor", cSC, "p1.vev.1.63")
                                P.wait("tensor", cSC, "p1.ev.k.1.31")
                        if itx >= 2:
                            P.wait("tensor", cSC, f"p2.exp.{itx - 2}")
                        P.inc(TE().matmul(pmB[six][:], Kb[hb][:, kkx * 128:(kkx + 1) * 128],
                                          Qb[hb][:, qb * 512:(qb + 1) * 512],
                                          start=True, stop=True), cPE, 1)
                        P.rec(f"p2.smm.{itx}", cPE)

                    def emit_exp(itx, kkx):
                        six = itx % 2
                        P.wait("scalar", cPE, f"p2.smm.{itx}")
                        if itx >= 4:
                            P.wait("scalar", cPE, f"p2.pv.{itx - 4}")
                        P.inc(SC().activation(Pt[itx % 4], pmB[six][:], AF.Exp, scale=ISQ), cSC, 1)
                        P.rec(f"p2.exp.{itx}", cSC)
                        if kkx >= 4 * qb:
                            rel = kkx - 4 * qb
                            if itx == 0:
                                P.wait("vector", None, "p2.mask")
                            P.wait("vector", cSC, f"p2.exp.{itx}")
                            P.inc(VE().tensor_tensor(Pt[itx % 4], Pt[itx % 4],
                                                     masks[:, rel * 512:(rel + 1) * 512], ALU.mult), cVE, 1)
                            P.rec(f"p2.msk.{itx}", cVE)

                    for j in range(min(2, kmax)):
                        emit_S(it0 + j, j)
                    if pending_endgame[0] is not None:
                        pending_endgame[0]()
                        pending_endgame[0] = None
                    for kk in range(kmax):
                        itx = it0 + kk
                        si = it % 2
                        masked = kk >= 4 * qb
                        emit_exp(itx, kk)
                        if kk == 0 and gqb >= 1:
                            P.wait("tensor", cVE, f"p2.acc.{gqb - 1}")
                        if masked:
                            P.wait("tensor", cVE, f"p2.msk.{itx}")
                        else:
                            P.wait("tensor", cSC, f"p2.exp.{itx}")
                        for qs in range(4):
                            inst = TE().matmul(pmA[qs][:, 0:129], Pt[itx % 4][:, qs * 128:(qs + 1) * 128],
                                               Vaug[hb][:, kk * 129:(kk + 1) * 129],
                                               start=(kk == 0), stop=(kk == kmax - 1))
                            if qs == 3:
                                P.inc(inst, cPE, 1)
                        P.rec(f"p2.pv.{itx}", cPE)
                        if kk + 2 < kmax:
                            emit_S(it0 + kk + 2, kk + 2)
                        it += 1
                    def mk_endgame(gqb=gqb, qb=qb, h=h, last_it=it - 1):
                        def endgame():
                            P.wait("vector", cPE, f"p2.pv.{last_it}")
                            if gqb >= 1:
                                P.wait("vector", cPE, f"p2.tr.{(gqb - 1) * 4 + 3}")
                            for qs in range(4):
                                VE().reciprocal(rl_sb[:, qs:qs + 1], pmA[qs][:, 128:129])
                                VE().tensor_scalar(accbf[qs], pmA[qs][:, 0:128],
                                                   rl_sb[:, qs:qs + 1], None, ALU.mult)
                            P.inc(vfence(accbf[3][:, 127:128]), cVE, 1)
                            P.rec(f"p2.acc.{gqb}", cVE)
                            for qs in range(4):
                                kq = gqb * 4 + qs
                                tb = qs % 2
                                if qs == 0:
                                    P.wait("tensor", cVE, f"p2.acc.{gqb}")
                                    if gqb == 0:
                                        P.wait("tensor", cSC, "p1.lnev.511")
                                if kq >= 2:
                                    P.wait("tensor", cVE, f"p2.trev.{kq - 2}")
                                P.inc(TE().transpose(pmT_bf[tb][:, 0:128], accbf[qs], ident[:]), cPE, 1)
                                P.rec(f"p2.tr.{kq}", cPE)
                                P.wait("vector", cPE, f"p2.tr.{kq}")
                                if qs == 0 and gqb >= 2:
                                    P.wait("vector", None, f"p2.ast.{gqb - 2}")
                                P.inc(VE().tensor_copy(atst[(qb % 2) * 4 + qs],
                                                       pmT_bf[tb][:, 0:128]), cVE, 1)
                                P.rec(f"p2.trev.{kq}", cVE)
                            P.wait("gpsimd", cVE, f"p2.trev.{gqb * 4 + 3}")
                            for qs in range(4):
                                dma(GP, cATS[qb % 2], atn_d[qb * 4 + qs, h], atst[(qb % 2) * 4 + qs])
                            P.rec(f"p2.ast.{gqb}", cATS[qb % 2])
                        return endgame
                    pending_endgame[0] = mk_endgame()
                P.rec(f"p2.hdone.{h}", cPE)
            if pending_endgame[0] is not None:
                pending_endgame[0]()
                pending_endgame[0] = None

        def p2b():
            P.wait("sync", cGP, "prep.wo")
            P.wait("sync", cPE, "p1.done.pe")
            for dc in range(16):
                P.inc(SY().dma_start_transpose(woT[dc], wob_d[:, dc * 128:(dc + 1) * 128]), C["WTL"], 16)
            P.rec("p2b.wt", C["WTL"])
            for c in range(NCH):
                P.wait("sync", cGP, f"p2.ast.{15 * 8 + c // 4}")
                if c < 2:
                    P.wait("sync", cPE, "p2.hdone.15")
                    P.wait("sync", cVE, "p2.msk.2303")
                    P.wait("sync", None, f"p1.xcons.{30 + c}")
                else:
                    P.wait("sync", cPE, f"p2b.mm.{(c - 2) * 4 + 3}")
                    P.wait("sync", None, f"p2b.st.{c - 2}")
                src = atn_d[c].rearrange("h p q -> p h q")
                dst = ab2[c % 2].rearrange("p (h q) -> p h q", h=16)
                for i in range(4):
                    dma(SY, cAB[c % 2], dst[:, i * 4:(i + 1) * 4, :], src[:, i * 4:(i + 1) * 4, :])
                P.rec(f"p2b.abld.{c}", cAB[c % 2])
                dma(SY, cX[c % 2], xbf[c % 2], x_d[c * 128:(c + 1) * 128, :])
                P.rec(f"p2b.xld.{c}", cX[c % 2])

                for hc in range(16):
                    if hc == 0:
                        P.wait("tensor", None, f"p2b.abld.{c}")
                        P.wait("tensor", None, f"p2b.xld.{c}")
                        if c == 0:
                            P.wait("tensor", None, "p2b.wt")
                            P.wait("tensor", cVE, "p2.acc.127")
                            P.wait("tensor", None, "p2.ast.127")
                        else:
                            P.wait("tensor", cVE, f"p2b.ev.{(c - 1) * 4 + 3}")
                    for s4 in range(4):
                        inst = TE().matmul(pmA[s4][:], ab2[c % 2][:, hc * 128:(hc + 1) * 128],
                                           woT[hc][:, s4 * 512:(s4 + 1) * 512],
                                           start=(hc == 0), stop=(hc == 15))
                        if s4 == 3 and hc == 15:
                            P.inc(inst, cPE, 1)
                P.rec(f"p2b.mm.{c * 4 + 3}", cPE)
                for s4 in range(4):
                    k = c * 4 + s4
                    if s4 == 0:
                        P.wait("vector", cPE, f"p2b.mm.{c * 4 + 3}")
                    P.inc(VE().tensor_tensor(xbf[c % 2][:, s4 * 512:(s4 + 1) * 512], pmA[s4][:],
                                             xbf[c % 2][:, s4 * 512:(s4 + 1) * 512], ALU.add), cVE, 1)
                    P.rec(f"p2b.ev.{k}", cVE)
                P.wait("gpsimd", cVE, f"p2b.ev.{c * 4 + 3}")
                dma(GP, cX1S[c % 2], x1_d[c * 128:(c + 1) * 128, :], xbf[c % 2])
                P.rec(f"p2b.st.{c}", cX1S[c % 2])
            P.rec("p2b.done.pe", cPE)

        def p3a():
            P.wait("sync", cPE, "p2b.done.pe")
            P.wait("sync", None, "p2b.st.30")
            P.wait("sync", None, "p2b.st.31")
            P.wait("gpsimd", cPE, "p2b.done.pe")
            for dc in range(16):
                if dc >= 2:
                    P.wait("sync", cVE, f"p3a.musqv.{dc - 2}")
                dma(SY, cX[dc % 2], xbf[dc % 2], mu_d[dc * 128:(dc + 1) * 128, :])
                P.rec(f"p3a.muf.{dc}", cX[dc % 2])
                dma(GP, C["MUB"], mun[dc], mu_d[dc * 128:(dc + 1) * 128, :])
                P.rec(f"p3a.mub.{dc}", C["MUB"])
                P.wait("vector", cLD, f"p3a.muf.{dc}")
                if dc >= 1:
                    P.wait("vector", cPE, f"p3a.musq.{dc - 1}")
                if dc == 0:
                    P.wait("vector", cPE, "p2b.done.pe")
                P.inc(VE().tensor_tensor(sqscr, xbf[dc % 2], xbf[dc % 2], ALU.mult), cVE, 1)
                P.rec(f"p3a.musqv.{dc}", cVE)
                P.wait("tensor", cVE, f"p3a.musqv.{dc}")
                if dc == 0:
                    P.wait("tensor", cVE, "p2.acc.127")
                    P.wait("tensor", cVE, "p2b.ev.127")
                    P.wait("tensor", cSC, "p2.exp.2303")
                for s4 in range(4):
                    tgt = (pmA[1], pmA[2], pmA[3], pmB[0])[s4]
                    inst = TE().matmul(tgt[0:1, :], onesc_f[:], sqscr[:, s4 * 512:(s4 + 1) * 512],
                                       start=(dc == 0), stop=(dc == 15))
                    if s4 == 3:
                        P.inc(inst, cPE, 1)
                P.rec(f"p3a.musq.{dc}", cPE)
            P.wait("scalar", cPE, "p3a.musq.15")
            for s4 in range(4):
                src = (pmA[1], pmA[2], pmA[3], pmB[0])[s4]
                SC().activation(rnrow[0:1, s4 * 512:(s4 + 1) * 512], src[0:1, :], AF.Ln)
                P.inc(SC().activation(rnrow[0:1, s4 * 512:(s4 + 1) * 512],
                                      rnrow[0:1, s4 * 512:(s4 + 1) * 512], AF.Exp, scale=0.5), cSC, 1)
            P.rec("p3a.nrm", cSC)
            P.wait("vector", cSC, "p3a.nrm")
            VE().tensor_scalar(rnrow[0:1, :], rnrow[0:1, :], 1e-12, None, ALU.max)
            VE().reciprocal(rnrow[0:1, :], rnrow[0:1, :])
            P.inc(VE().tensor_copy(stats[0:1, 31:32], rnrow[0:1, 2047:2048]), cVE, 1)
            P.rec("p3a.rn", cVE)
            P.wait("gpsimd", cVE, "p3a.rn")
            dma(GP, C["MUB"], rnh_d, rnrow[0:1, :])
            P.rec("p3a.rnst", C["MUB"])
            P.wait("sync", None, "p3a.rnst")
            dma(SY, C["BLK"], rnb, rnh_d.broadcast_to((128, 2048)))
            P.rec("p3a.rnld", C["BLK"])
            P.wait("vector", None, "p3a.rnld")
            P.wait("vector", None, "p3a.mub.15")
            for dc in range(16):
                P.inc(VE().tensor_tensor(mun[dc], mun[dc], rnb, ALU.mult), cVE, 1)
            P.rec("p3a.mun", cVE)
            P.wait("sync", cPE, "p3a.musq.15")
            dma(SY, C["CON"], Greg, gam_d.unsqueeze(0).broadcast_to((128, D)))
            dma(SY, C["CON"], Breg, bet_d.unsqueeze(0).broadcast_to((128, D)))
            P.rec("p3a.gb", C["CON"])

            for c in range(NCH):
                if c >= 2:
                    P.wait("sync", cVE, f"p3a.xcons.{c - 2}")
                else:
                    P.wait("sync", cVE, "p3a.mun")
                dma(SY, cX[c % 2], xbf[c % 2], x1_d[c * 128:(c + 1) * 128, :])
                P.rec(f"p3a.xld.{c}", cX[c % 2])

                P.wait("scalar", cLD, f"p3a.xld.{c}")
                P.inc(SC().activation(junkbf[:], xbf[c % 2], AF.Copy,
                                      accum_out=mv2[:, 0:1]), cSC, 1)
                P.rec(f"p3a.sx.{c}", cSC)

                P.wait("vector", cLD, f"p3a.xld.{c}")
                if c == 0:
                    P.wait("vector", cLD, "p3a.gb")
                if c >= 2:
                    P.wait("vector", cSC, f"p3a.rs.{c - 2}")
                vmargin()
                VE().tensor_tensor(lnscr, xbf[c % 2], xbf[c % 2], ALU.mult)
                VE().reduce_sum(mv2[:, 1:2], lnscr, AX.X)
                P.wait("vector", cSC, f"p3a.sx.{c}")
                VE().tensor_scalar(mv2[:, 0:1], mv2[:, 0:1], float(1.0 / D), None, ALU.mult)
                VE().tensor_scalar(mv2[:, 1:2], mv2[:, 1:2], float(1.0 / D), None, ALU.mult)
                VE().tensor_tensor(stats[:, 7:8], mv2[:, 0:1], mv2[:, 0:1], ALU.mult)
                VE().tensor_tensor(mv2[:, 1:2], mv2[:, 1:2], stats[:, 7:8], ALU.subtract)
                P.inc(vfence(mv2[:, 1:2]), cVE, 1)
                P.rec(f"p3a.mv.{c}", cVE)

                P.wait("scalar", cVE, f"p3a.mv.{c}")
                SC().activation(stats[:, 0:1], mv2[:, 1:2], AF.Ln, scale=float(1.0 + EPS),
                                bias=eps2_128[:])
                SC().activation(stats[:, 6:7], stats[:, 0:1], AF.Exp, scale=-0.5)
                P.inc(sfence(stats[:, 6:7]), cSC, 1)
                P.rec(f"p3a.rs.{c}", cSC)

                xtile = xlnres[c] if c < 20 else xlnst
                P.wait("vector", cSC, f"p3a.rs.{c}")
                if c == 0:
                    P.wait("vector", cPE, "p2b.done.pe")
                if c >= 20 and c >= 21:
                    P.wait("vector", cGP, f"p3a.xst.{c - 1}")
                VE().tensor_scalar(lnscr, xbf[c % 2], mv2[:, 0:1], stats[:, 6:7],
                                   ALU.subtract, ALU.mult)
                VE().tensor_tensor(lnscr, lnscr, Greg, ALU.mult)
                VE().tensor_tensor(xtile, lnscr, Breg, ALU.add)
                P.inc(vfence(xtile[:, 2047:2048]), cVE, 1)
                P.rec(f"p3a.xcons.{c}", cVE)

                VE().tensor_tensor(lnscr, xtile, xtile, ALU.mult)
                VE().reduce_sum(stats[:, 10:11], lnscr, AX.X)
                P.inc(vfence(stats[:, 10:11]), cVE, 1)
                P.rec(f"p3a.ssq.{c}", cVE)
                P.wait("scalar", cVE, f"p3a.ssq.{c}")
                SC().activation(stats[:, 11:12], stats[:, 10:11], AF.Ln)
                SC().activation(stats[:, 12:13], stats[:, 11:12], AF.Exp, scale=0.5)
                P.inc(sfence(stats[:, 12:13]), cSC, 1)
                P.rec(f"p3a.nr.{c}", cSC)
                P.wait("vector", cSC, f"p3a.nr.{c}")
                VE().tensor_scalar(stats[:, 13:14], stats[:, 12:13], 1e-12, None, ALU.max)
                VE().reciprocal(stats[:, 14:15], stats[:, 13:14])
                VE().tensor_tensor(stats[:, 15:16], stats[:, 14:15], sc_scl[:], ALU.mult)
                VE().tensor_scalar(xnbf, xtile, stats[:, 15:16], None, ALU.mult)
                P.inc(vfence(xnbf[:, 2047:2048]), cVE, 1)
                P.rec(f"p3a.xn.{c}", cVE)

                for dc in range(16):
                    k = c * 16 + dc
                    if dc == 0:
                        P.wait("tensor", cVE, f"p3a.xn.{c}")
                        if k == 0:
                            P.wait("tensor", cSC, "p2.trev.511")
                    if k >= 2:
                        P.wait("tensor", cSC, f"p3a.xnev.{k - 2}")
                    P.inc(TE().transpose(pmT_bf[k % 2][:, 0:128], xnbf[:, dc * 128:(dc + 1) * 128],
                                         ident[:]), cPE, 1)
                    P.rec(f"p3a.xtr.{k}", cPE)
                    P.wait("scalar", cPE, f"p3a.xtr.{k}")
                    if k == 0:
                        P.wait("scalar", cVE, "p3a.musqv.15")
                    P.inc(SC().copy(xnT[dc], pmT_bf[k % 2][:, 0:128]), cSC, 1)
                    P.rec(f"p3a.xnev.{k}", cSC)

                for dc in range(16):
                    if dc == 0:
                        P.wait("tensor", cSC, f"p3a.xnev.{c * 16 + 15}")
                        if c == 0:
                            P.wait("tensor", cVE, "p3a.mun")
                        else:
                            P.wait("tensor", cSC, f"p3a.eexp.{(c - 1) * 4 + 3}")
                    for s4 in range(4):
                        inst = TE().matmul(pmA[s4][:], xnT[dc], mun[dc][:, s4 * 512:(s4 + 1) * 512],
                                           start=(dc == 0), stop=(dc == 15))
                        if s4 == 3 and dc == 15:
                            P.inc(inst, cPE, 1)
                P.rec(f"p3a.lmm.{c * 4 + 3}", cPE)
                for s4 in range(4):
                    k = c * 4 + s4
                    if s4 == 0:
                        P.wait("scalar", cPE, f"p3a.lmm.{c * 4 + 3}")
                        if c >= 1:
                            P.wait("scalar", cGP, f"p3a.est.{c - 1}")
                            P.wait("scalar", cVE, f"p3a.res.{c - 1}")
                    P.inc(SC().activation(esb[:, s4 * 512:(s4 + 1) * 512], pmA[s4][:], AF.Exp,
                                          bias=sc_nab[:], accum_out=stats[:, 17 + s4:18 + s4]), cSC, 1)
                    P.rec(f"p3a.eexp.{k}", cSC)

                P.wait("vector", cSC, f"p3a.eexp.{c * 4 + 3}")
                VE().tensor_tensor(stats[:, 21:22], stats[:, 17:18], stats[:, 18:19], ALU.add)
                VE().tensor_tensor(stats[:, 22:23], stats[:, 19:20], stats[:, 20:21], ALU.add)
                VE().tensor_tensor(stats[:, 16:17], stats[:, 21:22], stats[:, 22:23], ALU.add)
                P.inc(VE().reciprocal(res_all[:, c:c + 1], stats[:, 16:17]), cVE, 1)
                P.rec(f"p3a.res.{c}", cVE)

                P.wait("gpsimd", cVE, f"p3a.xcons.{c}")
                dma(GP, C["XLS"], xln_d[c * 128:(c + 1) * 128, :], xtile)
                P.rec(f"p3a.xst.{c}", C["XLS"])
                P.wait("gpsimd", cVE, f"p3a.res.{c}")
                for ec in range(16):
                    dma(GP, C["EHS"], eh_d[ec, c * 128:(c + 1) * 128, :], esb[:, ec * 128:(ec + 1) * 128])
                P.rec(f"p3a.est.{c}", C["EHS"])

            P.rec("p3a.done.pe", cPE)

        def p3c():
            P.wait("sync", cPE, "p3a.done.pe")
            P.wait("sync", None, "p3a.est.31")
            for c in range(20, 32):
                dma(SY, C["BLK"], xlnres[c], xln_d[c * 128:(c + 1) * 128, :])
            P.rec("p3c.xln", C["BLK"])
            for e in range(NE):
                if e >= 1:
                    P.wait("sync", cPE, f"p3c.simm.{e - 1}")
                ebv = eb3.rearrange("p (c s) -> p c s", s=128)
                esrc = eh_d[e].rearrange("(c p) s -> p c s", p=128)
                for i in range(8):
                    dma(SY, C["EBL"], ebv[:, i * 4:(i + 1) * 4, :], esrc[:, i * 4:(i + 1) * 4, :])
                P.rec(f"p3c.ebld.{e}", C["EBL"])
                if e == 0:
                    P.wait("sync", cGP, "prep.w1")
                    P.wait("sync", cGP, "prep.w2")
                if e >= 1:
                    P.wait("sync", cVE, f"p3c.soev.{e - 1}")
                else:
                    P.wait("sync", cVE, "p3a.xcons.31")
                dma(SY, C["BLD"], B1f, b1_d[e:e + 1, :].broadcast_to((128, HID)))
                dma(SY, C["BLD"], B2f, b2_d[e:e + 1, :].broadcast_to((128, D)))
                P.rec(f"p3c.bld.{e}", C["BLD"])
                for qd in range(4):
                    kq = e * 4 + qd
                    if kq >= 2:
                        P.wait("sync", cPE, f"p3c.hmm.{kq - 2}")
                    w1src = w1b_d[e, qd * 512:(qd + 1) * 512, :].rearrange("(a p) h -> p a h", p=128)
                    dma(SY, cWQL[kq % 2], w1q[kq % 2].rearrange("p (a h) -> p a h", a=4), w1src)
                    P.rec(f"p3c.w1ld.{kq}", cWQL[kq % 2])
                for qd in range(4):
                    kq = e * 4 + qd
                    if kq >= 2:
                        P.wait("sync", cPE, f"p3c.somm.{kq - 2}")
                    dma(SY, cWQL[kq % 2], w2q[kq % 2], w2b_d[e, qd * 128:(qd + 1) * 128, :])
                    P.rec(f"p3c.w2ld.{kq}", cWQL[kq % 2])

                for c in range(NCH):
                    if c == 0:
                        P.wait("tensor", cLD, f"p3c.ebld.{e}")
                        if e == 0:
                            P.wait("tensor", cLD, "p3c.xln")
                            P.wait("tensor", cSC, "p3a.eexp.127")
                        else:
                            P.wait("tensor", cVE, f"p3c.soev.{e - 1}")
                    for s4 in range(4):
                        inst = TE().matmul(pmA[s4][:], eb3[:, c * 128:(c + 1) * 128],
                                           xlnres[c][:, s4 * 512:(s4 + 1) * 512],
                                           start=(c == 0), stop=(c == NCH - 1))
                        if s4 == 3 and c == NCH - 1:
                            P.inc(inst, cPE, 1)
                    TE().matmul(pmB[0][:, 0:1], eb3[:, c * 128:(c + 1) * 128], onesc_bf[:],
                                start=(c == 0), stop=(c == NCH - 1))
                P.rec(f"p3c.simm4.{e * 4 + 3}", cPE)
                for s4 in range(4):
                    k = e * 4 + s4
                    if s4 == 0:
                        P.wait("vector", cPE, f"p3c.simm4.{e * 4 + 3}")
                        if e == 0:
                            P.wait("vector", cSC, "p3a.xnev.511")
                        VE().reciprocal(rn_col[:, 0:1], pmB[0][:, 0:1])
                    P.inc(VE().tensor_scalar(slin[:, s4 * 512:(s4 + 1) * 512], pmA[s4][:],
                                             rn_col[:, 0:1], None, ALU.mult), cVE, 1)
                    P.rec(f"p3c.siev.{k}", cVE)
                P.rec(f"p3c.simm.{e}", cPE)

                for dc in range(16):
                    k = e * 16 + dc
                    if dc == 0:
                        P.wait("tensor", cVE, f"p3c.siev.{e * 4 + 3}")
                        if k == 0:
                            P.wait("tensor", cSC, "p3a.xnev.511")
                    if k >= 2:
                        P.wait("tensor", cSC, f"p3c.siTev.{k - 2}")
                    P.inc(TE().transpose(pmT_bf[k % 2][:, 0:128], slin[:, dc * 128:(dc + 1) * 128],
                                         ident[:]), cPE, 1)
                    P.rec(f"p3c.sitr.{k}", cPE)
                    P.wait("scalar", cPE, f"p3c.sitr.{k}")
                    P.inc(SC().copy(siT[dc], pmT_bf[k % 2][:, 0:128]), cSC, 1)
                    P.rec(f"p3c.siTev.{k}", cSC)

                for dc in range(16):
                    qd = dc // 4
                    kq = e * 4 + qd
                    if dc == 0:
                        P.wait("tensor", cSC, f"p3c.siTev.{e * 16 + 15}")
                        if e >= 1:
                            P.wait("tensor", cVE, f"p3c.hpre.{e - 1}")
                    if dc % 4 == 0:
                        P.wait("tensor", cLD, f"p3c.w1ld.{kq}")
                    inst = TE().matmul(pmB[0][:], siT[dc], w1q[kq % 2][:, (dc % 4) * 512:(dc % 4 + 1) * 512],
                                       start=(dc == 0), stop=(dc == 15))
                    if dc % 4 == 3:
                        P.inc(inst, cPE, 1)
                        P.rec(f"p3c.hmm.{kq}", cPE)
                P.rec(f"p3c.hdone.{e}", cPE)
                P.wait("vector", cPE, f"p3c.hdone.{e}")
                P.wait("vector", None, f"p3c.bld.{e}")
                P.inc(VE().tensor_tensor(hpre, pmB[0][:], B1f, ALU.add), cVE, 1)
                P.rec(f"p3c.hpre.{e}", cVE)
                P.wait("scalar", cVE, f"p3c.hpre.{e}")
                P.inc(SC().activation(hbf, hpre, AF.Gelu), cSC, 1)
                P.rec(f"p3c.gelu.{e}", cSC)

                for i in range(4):
                    k = e * 4 + i
                    if i == 0:
                        P.wait("tensor", cSC, f"p3c.gelu.{e}")
                    if k >= 2:
                        P.wait("tensor", cSC, f"p3c.hTev.{k - 2}")
                    P.inc(TE().transpose(pmT_bf[k % 2][:, 0:128], hbf[:, i * 128:(i + 1) * 128],
                                         ident[:]), cPE, 1)
                    P.rec(f"p3c.htr.{k}", cPE)
                    P.wait("scalar", cPE, f"p3c.htr.{k}")
                    P.inc(SC().copy(hT[i], pmT_bf[k % 2][:, 0:128]), cSC, 1)
                    P.rec(f"p3c.hTev.{k}", cSC)

                for hc in range(4):
                    kq = e * 4 + hc
                    if hc == 0:
                        P.wait("tensor", cSC, f"p3c.hTev.{e * 4 + 3}")
                    P.wait("tensor", cLD, f"p3c.w2ld.{kq}")
                    for s4 in range(4):
                        inst = TE().matmul(pmA[s4][:], hT[hc], w2q[kq % 2][:, s4 * 512:(s4 + 1) * 512],
                                           start=(hc == 0), stop=(hc == 3))
                        if s4 == 3:
                            P.inc(inst, cPE, 1)
                    P.rec(f"p3c.somm.{kq}", cPE)
                P.rec(f"p3c.sodone.{e}", cPE)

                P.wait("vector", cPE, f"p3c.sodone.{e}")
                if e >= 1:
                    P.wait("vector", None, f"p3c.sost.{e - 1}")
                for s4 in range(4):
                    inst = VE().tensor_tensor(sobf[:, s4 * 512:(s4 + 1) * 512], pmA[s4][:],
                                              B2f[:, s4 * 512:(s4 + 1) * 512], ALU.add)
                    if s4 == 3:
                        P.inc(inst, cVE, 1)
                P.rec(f"p3c.soev.{e}", cVE)
                P.wait("gpsimd", cVE, f"p3c.soev.{e}")
                dma(GP, C["SOS"], soh_d[e], sobf)
                P.rec(f"p3c.sost.{e}", C["SOS"])
            P.rec("p3c.done.pe", cPE)

        def p3e():
            P.wait("sync", cPE, "p3c.done.pe")
            P.wait("sync", None, "p3c.sost.15")
            for ec in range(16):
                dma(SY, C["BLK"], sores[ec], soh_d[ec])
            P.rec("p3e.sold", C["BLK"])
            for g4 in range(4):
                if g4 >= 2:
                    P.wait("sync", cPE, f"p3e.gdone.{g4 - 2}")
                for ec in range(16):
                    P.inc(SY().dma_start_transpose(ebT[g4 % 2][ec],
                                                   eh_d[ec, g4 * 1024:(g4 + 1) * 1024, :]), cEBT[g4 % 2], 16)
                P.rec(f"p3e.ebt.{g4}", cEBT[g4 % 2])
                for cc in range(8):
                    c = g4 * 8 + cc
                    if c < 2:
                        P.wait("sync", cVE, f"p3a.xcons.{30 + c}")
                        P.wait("sync", cVE, "p3c.soev.15")
                    else:
                        P.wait("sync", cVE, f"p3e.sev.{(c - 2) * 4 + 3}")
                    dma(SY, cX[c % 2], xbf[c % 2], x1_d[c * 128:(c + 1) * 128, :])
                    P.rec(f"p3e.xld.{c}", cX[c % 2])
                    for ec in range(16):
                        if ec == 0:
                            P.wait("tensor", cLD, f"p3e.xld.{c}")
                            P.wait("tensor", cLD, f"p3e.ebt.{g4}")
                            if c == 0:
                                P.wait("tensor", cLD, "p3e.sold")
                                P.wait("tensor", cVE, "p3c.soev.15")
                            else:
                                P.wait("tensor", cVE, f"p3e.sev.{(c - 1) * 4 + 3}")
                        for s4 in range(4):
                            inst = TE().matmul(pmA[s4][:], ebT[g4 % 2][ec][:, cc * 128:(cc + 1) * 128],
                                               sores[ec][:, s4 * 512:(s4 + 1) * 512],
                                               start=(ec == 0), stop=(ec == 15))
                            if s4 == 3 and ec == 15:
                                P.inc(inst, cPE, 1)
                    P.rec(f"p3e.mm.{c * 4 + 3}", cPE)
                    for s4 in range(4):
                        k = c * 4 + s4
                        if s4 == 0:
                            P.wait("vector", cPE, f"p3e.mm.{c * 4 + 3}")
                            if c >= 2:
                                P.wait("vector", None, f"p3e.ost.{c - 2}")
                        P.inc(VE().scalar_tensor_tensor(osb[c % 2][:, s4 * 512:(s4 + 1) * 512],
                                                        pmA[s4][:], res_all[:, c:c + 1],
                                                        xbf[c % 2][:, s4 * 512:(s4 + 1) * 512],
                                                        ALU.mult, ALU.add), cVE, 1)
                        P.rec(f"p3e.sev.{k}", cVE)
                    P.wait("gpsimd", cVE, f"p3e.sev.{c * 4 + 3}")
                    dma(GP, cOUT[c % 2], out_d[c * 128:(c + 1) * 128, :], osb[c % 2])
                    P.rec(f"p3e.ost.{c}", cOUT[c % 2])
                P.rec(f"p3e.gdone.{g4}", cPE)
            P.wait("gpsimd", None, "p3e.ost.30")
            P.wait("gpsimd", None, "p3e.ost.31")

        def program():
            phases = (("p0", p0_prep), ("p1", p1), ("p2", p2), ("p2b", p2b),
                      ("p3a", p3a), ("p3c", p3c), ("p3e", p3e))
            for name, ph in phases[:upto]:
                with nc.named_scope(name):
                    ph()

        P.begin("count")
        program()

        with nc.Block() as blk0:
            @blk0.sync
            def _(eng):
                for s in sems.values():
                    nc.sync.sem_clear(s)

        with nc.Block() as block:
            @block.sync
            def _(eng):
                P.begin("sync")
                program()

            @block.gpsimd
            def _(eng):
                P.begin("gpsimd")
                program()

            @block.tensor
            def _(eng):
                P.begin("tensor")
                program()

            @block.vector
            def _(eng):
                P.begin("vector")
                program()

            @block.scalar
            def _(eng):
                P.begin("scalar")
                program()

    return nc


def kernel(**inputs):
    x = np.asarray(inputs["L0_attn_x"], dtype=np.float32)
    C, S = _rope_tables()
    common = {
        "wq": np.ascontiguousarray(np.asarray(inputs["L0_attn_Wq"], dtype=np.float32)),
        "wk": np.ascontiguousarray(np.asarray(inputs["L0_attn_Wk"], dtype=np.float32)),
        "wv": np.ascontiguousarray(np.asarray(inputs["L0_attn_Wv"], dtype=np.float32)),
        "wo": np.ascontiguousarray(np.asarray(inputs["L0_attn_Wo"], dtype=np.float32)),
        "gam": np.ascontiguousarray(np.asarray(inputs["L0_ffn_gamma"], dtype=np.float32)),
        "bet": np.ascontiguousarray(np.asarray(inputs["L0_ffn_beta"], dtype=np.float32)),
        "mu": np.ascontiguousarray(np.asarray(inputs["L0_ffn_mu"], dtype=np.float32).reshape(D, NE * NS)),
        "scl": np.asarray(inputs["L0_ffn_scale"], dtype=np.float32).reshape(1, 1),
        "w1": np.ascontiguousarray(np.asarray(inputs["L0_ffn_W1"], dtype=np.float32)),
        "b1": np.ascontiguousarray(np.asarray(inputs["L0_ffn_b1"], dtype=np.float32)),
        "w2": np.ascontiguousarray(np.asarray(inputs["L0_ffn_W2"], dtype=np.float32)),
        "b2": np.ascontiguousarray(np.asarray(inputs["L0_ffn_b2"], dtype=np.float32)),
        "ident": np.eye(128, dtype=np.float32).astype(BF),
        "ropec": C, "ropes": S, "mask": _mask_tiles(),
    }
    nc = build()
    in_maps = [dict(common, x=np.ascontiguousarray(x[b])) for b in range(8)]
    res = run_bass_kernel_spmd(nc, in_maps, core_ids=list(range(8)))
    return np.stack([np.asarray(r["out"], dtype=np.float32) for r in res.results], axis=0)
